# version 17
# baseline (speedup 1.0000x reference)
import contextlib
import numpy as np
import concourse.bass as bass
import concourse.mybir as mybir
from concourse.bass_utils import run_bass_kernel_spmd

F32 = mybir.dt.float32
BF16 = mybir.dt.bfloat16
I32 = mybir.dt.int32
AF = mybir.ActivationFunctionType
ALU = mybir.AluOpType
AX = mybir.AxisListType

f32 = np.float32


class Cfg:
    def __init__(self, SEQ=4096, DM=1024, NH=16, E=64, G=8, TG=4, TK=6,
                 INTER=512, NQUARTER=4):
        self.SEQ, self.DM, self.NH, self.HD = SEQ, DM, NH, 64
        self.E, self.G, self.TG, self.TK = E, G, TG, TK
        self.INTER, self.ISH = INTER, 2 * INTER
        self.NQUARTER = NQUARTER
        self.QUARTER = SEQ // NQUARTER
        assert SEQ % NQUARTER == 0 and self.QUARTER % 128 == 0
        self.NTQ = self.QUARTER // 128
        self.NTILES = SEQ // 128
        self.NCF = DM // 128
        self.NCI = INTER // 128
        assert NH * self.HD == DM and DM % 128 == 0 and INTER % 128 == 0
        self.HM = min(4, NH)
        self.HP = min(8, NH)
        assert NH % self.HP == 0 and self.HP % self.HM == 0
        self.NPASS = NH // self.HP
        self.NMM = self.HP // self.HM
        self.HQ = min(512, self.QUARTER)
        self.NHQ = self.QUARTER // self.HQ
        self.TB = min(512, self.QUARTER)
        self.NTB = self.QUARTER // self.TB
        self.NTT = self.TB // 128
        self.DH = min(512, DM)
        self.NDH = DM // self.DH
        self.SLOT = self.NCF * INTER
        assert self.SLOT == self.NCI * DM
        self.EPG = E // G
        self.NSLOT = self.E + 2


FULL = Cfg()


def build(c: Cfg):
    nc = bass.Bass("TRN2", target_bir_lowering=False)

    x_d = nc.dram_tensor("x", [c.SEQ, c.DM], F32, kind="ExternalInput").ap()
    wq_d = nc.dram_tensor("wq", [c.DM, c.DM], F32, kind="ExternalInput").ap()
    wk_d = nc.dram_tensor("wk", [c.DM, c.HD], F32, kind="ExternalInput").ap()
    wv_d = nc.dram_tensor("wv", [c.DM, c.HD], F32, kind="ExternalInput").ap()
    wo_d = nc.dram_tensor("wo", [c.DM, c.DM], F32, kind="ExternalInput").ap()
    bo_d = nc.dram_tensor("bo", [1, c.DM], F32, kind="ExternalInput").ap()
    wgate_d = nc.dram_tensor("wgate", [c.DM, c.E], F32, kind="ExternalInput").ap()
    bgate_d = nc.dram_tensor("bgate", [1, c.E], F32, kind="ExternalInput").ap()
    weg_d = nc.dram_tensor("weg", [c.E, c.DM, c.INTER], F32, kind="ExternalInput").ap()
    weu_d = nc.dram_tensor("weu", [c.E, c.DM, c.INTER], F32, kind="ExternalInput").ap()
    wed_d = nc.dram_tensor("wed", [c.E, c.INTER, c.DM], F32, kind="ExternalInput").ap()
    wsg_d = nc.dram_tensor("wsg", [c.DM, c.ISH], F32, kind="ExternalInput").ap()
    wsu_d = nc.dram_tensor("wsu", [c.DM, c.ISH], F32, kind="ExternalInput").ap()
    wsd_d = nc.dram_tensor("wsd", [c.ISH, c.DM], F32, kind="ExternalInput").ap()
    out_d = nc.dram_tensor("out", [c.SEQ, c.DM], F32, kind="ExternalOutput").ap()

    with contextlib.ExitStack() as ctx:
        e_ = ctx.enter_context

        x_res = e_(nc.sbuf_tensor("x_res", [128, c.NTQ * c.DM], F32))
        kprime = e_(nc.sbuf_tensor("kprime", [128, c.SEQ], BF16))
        vprime = e_(nc.sbuf_tensor("vprime", [128, c.NTILES * 66], BF16))
        racc = e_(nc.sbuf_tensor("racc", [128, c.NTQ * c.DM], F32))
        arena = e_(nc.sbuf_tensor("arena", [128, 6 * c.SLOT], BF16))
        nbig = max(c.NH * c.HQ, c.NCF * c.QUARTER)
        bigbf = e_(nc.sbuf_tensor("bigbf", [128, nbig], BF16))
        nhp = max(2 * c.HP * 128, 2 * c.NCI * c.TB)
        hp_ar = e_(nc.sbuf_tensor("hp_ar", [128, nhp], BF16))
        hg = e_(nc.sbuf_tensor("hg", [128, max(c.NCI * c.TB, c.NH * 64)], F32))
        xnq_fm = e_(nc.sbuf_tensor("xnq_fm", [128, max(c.NCF * c.HQ, 2 * c.NCF * 128)], BF16))
        xn_tm = e_(nc.sbuf_tensor("xn_tm", [128, 2 * c.DM], BF16))
        out_sb = e_(nc.sbuf_tensor("out_sb", [128, 2 * c.DM], F32))
        nrm = e_(nc.sbuf_tensor("nrm", [128, c.DM], F32))
        comb = e_(nc.sbuf_tensor("comb", [128, c.NTQ * c.E], F32))
        sc_r = e_(nc.sbuf_tensor("sc_r", [128, c.NTQ * c.E], F32))
        gw_r = e_(nc.sbuf_tensor("gw_r", [128, c.NTQ * c.E], F32))
        gs_r = e_(nc.sbuf_tensor("gs_r", [128, c.NTQ * c.G], F32))
        gg_r = e_(nc.sbuf_tensor("gg_r", [128, c.NTQ * c.G], F32))
        mcol = e_(nc.sbuf_tensor("mcol", [128, c.NTQ], F32))
        rb_sb = e_(nc.sbuf_tensor("rb_sb", [64, c.HP * 128], F32))
        av_scr = e_(nc.sbuf_tensor("av_scr", [65, 2 * c.HP * 128], BF16))
        r65 = e_(nc.sbuf_tensor("r65", [65, c.HP * 128], BF16))
        ones65r = e_(nc.sbuf_tensor("ones65r", [65, 64], BF16))
        q2m = e_(nc.sbuf_tensor("q2m", [64, c.NH], BF16))
        onecol_b = e_(nc.sbuf_tensor("onecol_b", [128, 1], BF16))
        nb_bc = e_(nc.sbuf_tensor("nb_bc", [128, 1], F32))
        q2sb = e_(nc.sbuf_tensor("q2sb", [64, max(c.HQ, 128)], BF16))
        bb65 = e_(nc.sbuf_tensor("bb65", [1, max(c.NH, c.NTILES) + 4], F32))
        mask_rep = e_(nc.sbuf_tensor("mask_rep", [128, c.HP * 128], BF16))
        iota_sb = e_(nc.sbuf_tensor("iota_sb", [128, 128], I32))
        ident_bf = e_(nc.sbuf_tensor("ident_bf", [128, 128], BF16))
        wk_sb = e_(nc.sbuf_tensor("wk_sb", [128, c.NCF * c.HD], BF16))
        wv_sb = e_(nc.sbuf_tensor("wv_sb", [128, c.NCF * c.HD], BF16))
        gate_sb = e_(nc.sbuf_tensor("gate_sb", [128, c.NCF * c.E], BF16))
        bg_bc = e_(nc.sbuf_tensor("bg_bc", [128, c.E], F32))
        ones_r = e_(nc.sbuf_tensor("ones_r", [128, c.EPG], F32))
        ones01 = e_(nc.sbuf_tensor("ones01", [1, 128], F32))
        ones1b = e_(nc.sbuf_tensor("ones1b", [1, 128], BF16))
        bo_bf = e_(nc.sbuf_tensor("bo_bf", [1, c.DM], BF16))
        onesb65 = e_(nc.sbuf_tensor("onesb65", [64, 1], BF16))
        stats = e_(nc.sbuf_tensor("stats", [128, 4 * 12], F32))
        mv = e_(nc.sbuf_tensor("mv", [128, 4 * 2], F32))
        rstd = e_(nc.sbuf_tensor("rstd", [128, 4], F32))

        psA = e_(nc.psum_tensor("psA", [128, 1024], F32))
        psB = e_(nc.psum_tensor("psB", [128, 1024], F32))
        psC = e_(nc.psum_tensor("psC", [128, 1024], F32))
        psD = e_(nc.psum_tensor("psD", [128, 512], F32))
        psT = e_(nc.psum_tensor("psT", [128, 512], F32))

        sem_names = ["pe", "act", "dve", "pool",
                     "sx0", "sx1", "sxq", "srow", "sr", "so0", "so1",
                     "swinit", "sA", "sE0", "sE1"]
        sems = {n: e_(nc.semaphore(n)) for n in sem_names}
        cnt = {n: 0 for n in sem_names}
        prog = {n: [] for n in ["pe", "act", "dve", "pool", "sync"]}
        last_wait = {}

        def emit(en, fn, waits=(), inc=None, amt=1, lp=False):
            ws = []
            for (s, v) in waits:
                if v <= 0:
                    continue
                key = (en, s)
                if v > last_wait.get(key, -1):
                    ws.append((s, v))
                    last_wait[key] = v
            prog[en].append((ws, (fn, lp) if lp else fn, inc, amt))
            if inc is not None:
                cnt[inc] += amt
                return cnt[inc]
            return None

        def drain(en):
            prog[en].append(((), "drain", None, 0))

        def finish():
            with nc.Block() as block:
                def mk(en_name):
                    def body(eng):
                        for (ws, fn, inc, amt) in prog[en_name]:
                            for (s_, v_) in ws:
                                eng.wait_ge(sems[s_], v_)
                            if fn == "drain":
                                eng.drain()
                                continue
                            if isinstance(fn, tuple):
                                with nc.allow_low_precision(reason="bf16 softmax denominator (l already bf16)"):
                                    inst = fn[0](eng)
                            else:
                                inst = fn(eng)
                            if inc is not None:
                                inst.then_inc(sems[inc], amt)
                    return body
                block.tensor(mk("pe"))
                block.scalar(mk("act"))
                block.vector(mk("dve"))
                block.gpsimd(mk("pool"))
                block.sync(mk("sync"))
            return nc

        def RD():
            return {"pe": 0, "act": 0, "dve": 0}

        def oz():
            return racc[:, 0:c.DM]

        def psTb():
            return psT[0:128, 0:512].bitcast(BF16)

        def bg_row():
            return rb_sb[0:1, 0:c.E]

        def attn_dm():
            return hg[0:64, 0:c.NH * 64].bitcast(BF16)

        def rwait(*trs):
            w = []
            for tr in trs:
                for k, v in tr.items():
                    if v:
                        w.append((k, v))
            return w

        def rset(tr, eng, v):
            if v is not None and v > tr.get(eng, 0):
                tr[eng] = v

        rd_psA = RD()
        rd_psB = RD()
        rd_psC = [RD(), RD()]
        rd_psD = RD()
        rd_psT = RD()
        rd_P = [RD(), RD()]
        rd_xn = [RD(), RD()]
        rd_xnq = RD()
        rd_big = RD()
        rd_nrm = RD()
        rd_hg = [RD() for _ in range(max(c.NCI, 1))]
        rd_attn = RD()
        rd_avs = [RD(), RD()]
        rd_rb = RD()
        rd_racc = RD()
        rd_q2 = RD()

        ln_par = [0]

        def ln_tile(src_fn, dst_fn, dve_waits=(), act_waits=()):
            par = ln_par[0]
            ln_par[0] = (ln_par[0] + 1) % 4
            nhalf = (c.DM + 511) // 512
            for i in range(nhalf):
                lo, hi = i * 512, min((i + 1) * 512, c.DM)
                emit("dve", (lambda i=i, lo=lo, hi=hi, par=par:
                             (lambda v: v.bn_stats(stats[:, par * 12 + i * 6:par * 12 + (i + 1) * 6],
                                                   src_fn()[:, lo:hi])))(),
                     waits=(dve_waits if i == 0 else ()))
            drain("dve")
            dv = emit("dve", (lambda par=par, nhalf=nhalf:
                              (lambda v: v.bn_aggr(mv[:, par * 2:par * 2 + 2],
                                                   stats[:, par * 12:par * 12 + 6 * nhalf])))(),
                      inc="dve")
            av = emit("act", (lambda par=par: (lambda s: s.activation(
                rstd[:, par:par + 1], mv[:, par * 2 + 1:par * 2 + 2], AF.Sqrt,
                bias=f32(1e-5))))(), waits=[("dve", dv)], inc="act")
            emit("dve", (lambda par=par: (lambda v: v.reciprocal(
                rstd[:, par:par + 1], rstd[:, par:par + 1])))(), waits=[("act", av)])
            drain("dve")
            aev = emit("dve", (lambda par=par: (lambda v: v.tensor_scalar(
                dst_fn(), src_fn(), mv[:, par * 2:par * 2 + 1],
                rstd[:, par:par + 1], op0=ALU.subtract, op1=ALU.mult)))(),
                waits=list(act_waits), inc="dve")
            return aev

        def transpose_tile(src_fn, dst_fn, pe_waits, dve_waits=()):
            last = None
            for ch in range(c.NCF):
                w = list(pe_waits) + rwait(rd_psT) if ch == 0 else ()
                last = emit("pe", (lambda ch=ch: (lambda t: t.transpose(
                    psTb()[:, ch * 128:(ch + 1) * 128],
                    src_fn()[:, ch * 128:(ch + 1) * 128], ident_bf[:, :])))(),
                    waits=w, inc="pe")
            dv = emit("dve", (lambda: (lambda v: v.tensor_copy(
                dst_fn(), psTb()[:, 0:c.NCF * 128]
                .rearrange("p (ch t) -> p ch t", ch=c.NCF))))(),
                waits=list(dve_waits) + [("pe", last)], inc="dve")
            rset(rd_psT, "dve", dv)
            return dv, last

        p_iota = emit("pool", lambda g: g.iota(iota_sb[:, :], [[-1, 128]], base=0,
                                               channel_multiplier=1), inc="pool")
        emit("dve", lambda v: v.tensor_scalar(ident_bf[:, :], iota_sb[:, :], 0.0, None,
                                              op0=ALU.is_equal),
             waits=[("pool", p_iota)])
        for h in range(c.HP):
            emit("dve", (lambda h=h: (lambda v: v.tensor_scalar(
                mask_rep[:, h * 128:(h + 1) * 128], iota_sb[:, :], 0.0, None,
                op0=ALU.is_le)))())
        emit("dve", lambda v: v.memset(vprime[:, :], 1.0))
        emit("dve", lambda v: v.memset(kprime[64:65, :], -1.0))
        emit("dve", lambda v: v.memset(ones01[:, :], 1.0))
        emit("dve", lambda v: v.memset(ones1b[:, :], 1.0))
        emit("dve", lambda v: v.memset(onecol_b[:, :], 1.0))
        emit("dve", lambda v: v.memset(ones65r[:, :], 1.0))
        emit("dve", lambda v: v.memset(ones_r[:, :], 1.0))
        drain("dve")
        dv_init = emit("dve", lambda v: v.memset(onesb65[:, 0:1], 1.0), inc="dve")

        emit("pool", lambda g: g.dma_start(
            wk_sb[:, :].rearrange("p (m d) -> p m d", m=c.NCF),
            wk_d.rearrange("(m p) d -> p m d", p=128)), inc="swinit", amt=16)
        emit("pool", lambda g: g.dma_start(
            wv_sb[:, :].rearrange("p (m d) -> p m d", m=c.NCF),
            wv_d.rearrange("(m p) d -> p m d", p=128)), inc="swinit", amt=16)
        emit("pool", lambda g: g.dma_start(
            gate_sb[:, :].rearrange("p (m d) -> p m d", m=c.NCF),
            wgate_d.rearrange("(m p) d -> p m d", p=128)), inc="swinit", amt=16)
        emit("pool", lambda g: g.dma_start(bo_bf[0:1, :], bo_d), inc="swinit", amt=16)
        sw_small = cnt["swinit"]
        emit("sync", lambda sy: sy.dma_start(bg_row(), bgate_d), inc="srow", amt=16)
        sx_rows = cnt["srow"]

        pc = emit("pe", lambda t: t.matmul(psD[0:128, 0:c.E], ones01[0:1, 0:128],
                                           bg_row(), start=True, stop=True),
                  waits=[("srow", sx_rows), ("dve", dv_init)] + rwait(rd_psD), inc="pe")
        rset(rd_psD, "dve", emit("dve", lambda v: v.tensor_copy(bg_bc[:, :], psD[0:128, 0:c.E]),
                                 waits=[("pe", pc)], inc="dve"))

        dbg = getattr(c, "dbg", 0)
        skips = getattr(c, "skips", set())
        if dbg == 1:
            emit("sync", lambda sy: sy.nop(), waits=[("srow", cnt["srow"])])
            return finish()
        kv_ln = {}
        for t in range(c.NTILES):
            par = t % 2
            sxn = "sx" + str(par)
            dw = [("dve", kv_ln[t - 2])] if t >= 2 else []
            emit("sync", (lambda t=t, par=par: (lambda sy: sy.dma_start(
                out_sb[:, par * c.DM:(par + 1) * c.DM], x_d[t * 128:(t + 1) * 128, :])))(),
                waits=dw, inc=sxn, amt=16)
            sx_now = cnt[sxn]
            kv_ln[t] = ln_tile((lambda par=par: out_sb[:, par * c.DM:(par + 1) * c.DM]),
                               (lambda par=par: xn_tm[:, par * c.DM:(par + 1) * c.DM]),
                               dve_waits=[(sxn, sx_now)],
                               act_waits=rwait(rd_xn[par]))
            fmo = par * c.NCF * 128
            dvt, tr_pe = transpose_tile(
                (lambda par=par: xn_tm[:, par * c.DM:(par + 1) * c.DM]),
                (lambda fmo=fmo: xnq_fm[:, fmo:fmo + c.NCF * 128]
                 .rearrange("p (ch t) -> p ch t", ch=c.NCF)),
                pe_waits=[("dve", kv_ln[t])],
                dve_waits=rwait(rd_xnq))
            rset(rd_xn[par], "pe", tr_pe)
            lastk = None
            for ch in range(c.NCF):
                w = []
                if ch == 0:
                    w = [("dve", dvt), ("swinit", sw_small)] + rwait(rd_psD)
                lastk = emit("pe", (lambda ch=ch, fmo=fmo: (lambda t_: t_.matmul(
                    psD[0:64, 0:128], wk_sb[:, ch * c.HD:(ch + 1) * c.HD],
                    xnq_fm[:, fmo + ch * 128:fmo + (ch + 1) * 128],
                    start=(ch == 0), stop=(ch == c.NCF - 1))))(), waits=w, inc="pe")
            dvk = emit("dve", (lambda t=t: (lambda v: v.tensor_copy(
                kprime[0:64, t * 128:(t + 1) * 128], psD[0:64, 0:128])))(),
                waits=[("pe", lastk)], inc="dve")
            rset(rd_psD, "dve", dvk)
            if "k2" not in skips:
                a2 = emit("act", lambda s: s.activation(q2sb[0:64, 0:128], psD[0:64, 0:128],
                                                        AF.Square),
                          waits=[("pe", lastk), ("dve", dvk)] + rwait(rd_q2), inc="act")
                rset(rd_psD, "act", a2)
                if "bmm" in skips:
                    pk2 = None
                else:
                    pk2 = emit("pe", lambda t_: t_.matmul(psD[0:1, 256:384], onesb65[:, :],
                                                          q2sb[0:64, 0:128], start=True, stop=True),
                               waits=[("act", a2), ("dve", dvk)], inc="pe")
                    rset(rd_q2, "pe", pk2)
                if "red64" not in skips and pk2 is not None:
                    rset(rd_psD, "dve", emit("dve", (lambda t=t: (lambda v: v.reduce_max(
                        bb65[0:1, t:t + 1], psD[0:1, 256:384], axis=AX.X)))(),
                        waits=[("pe", pk2)], inc="dve"))
            lastv = None
            for ch in range(c.NCF):
                w = rwait(rd_psA) if ch == 0 else ()
                lastv = emit("pe", (lambda ch=ch, fmo=fmo: (lambda t_: t_.matmul(
                    psA[0:128, 0:64], xnq_fm[:, fmo + ch * 128:fmo + (ch + 1) * 128],
                    wv_sb[:, ch * c.HD:(ch + 1) * c.HD],
                    start=(ch == 0), stop=(ch == c.NCF - 1))))(), waits=w, inc="pe")
            rset(rd_xnq, "pe", lastv)
            rset(rd_psA, "dve", emit("dve", (lambda t=t: (lambda v: v.tensor_copy(
                vprime[:, t * 66:t * 66 + 64], psA[0:128, 0:64])))(),
                waits=[("pe", lastv)], inc="dve"))
        drain("dve")
        emit("sync", lambda sy: sy.dma_start(kprime[64:128, 0:c.SEQ], kprime[0:64, 0:c.SEQ]),
             waits=[("dve", cnt["dve"])], inc="sr", amt=16)
        st_kdup = cnt["sr"]
        KMAXC = max(c.NH, c.NTILES)
        if "k2" not in skips and "red64" not in skips:
            dv_kmax = emit("dve", lambda v: v.reduce_max(bb65[0:1, KMAXC:KMAXC + 1],
                                                         bb65[0:1, 0:c.NTILES], axis=AX.X),
                           inc="dve")
        else:
            dv_kmax = 0

        if dbg == 2:
            emit("sync", lambda sy: sy.nop(), waits=[("srow", cnt["srow"]), ("dve", cnt["dve"])])
            return finish()
        def qprime():
            return bigbf[0:65, 0:c.NH * c.HQ]

        def xn2_fm():
            return bigbf[0:128, 0:c.NCF * c.QUARTER]

        def wq_ap():
            return arena[:, 0:c.NCF * c.DM]

        def wo_ap():
            return arena[0:64, c.NCF * c.DM:c.NCF * c.DM + c.NH * c.DM]

        def eslot(s):
            return arena[:, s * c.SLOT:(s + 1) * c.SLOT]

        st = {
            "expert_pe": 0, "oproj_pe": 0, "qproj_pe": 0,
            "xres_free": 0, "eloads": [0, 0], "sr_n": 0, "r_pe": pc,
            "out_so": {0: 0, 1: 0}, "comb_done": 0, "qdup_rd": 0,
            "fin": [],
        }

        def flush_fin():
            fins = st["fin"]
            st["fin"] = []
            for f in fins:
                f()
        xres_add = {}
        xn2_dve = {}

        for q in range(c.NQUARTER):
            QT0 = q * c.NTQ

            ww = [("pe", st["expert_pe"])] if q > 0 else [("swinit", sw_small)]
            emit("pool", lambda g: g.dma_start(
                wq_ap().rearrange("p (m d) -> p m d", m=c.NCF),
                wq_d.rearrange("(m p) d -> p m d", p=128)), waits=ww, inc="sA", amt=16)
            emit("pool", lambda g: g.dma_start(
                wo_ap().rearrange("p (h d) -> p h d", h=c.NH),
                wo_d.rearrange("(h p) d -> p h d", p=64)), inc="sA", amt=16)
            sA_now = cnt["sA"]

            xw = [("dve", st["xres_free"])] if q > 0 else []
            for j in range(c.NTQ):
                gt = QT0 + j
                emit("sync", (lambda j=j, gt=gt: (lambda sy: sy.dma_start(
                    x_res[:, j * c.DM:(j + 1) * c.DM], x_d[gt * 128:(gt + 1) * 128, :])))(),
                    waits=(xw if j == 0 else ()), inc="sxq", amt=16)
            sx_xq = cnt["sxq"]

            for hb in range(c.NHQ):
                flush_fin()
                jt0 = hb * (c.HQ // 128)
                last_tr_dve = 0
                pend_t = None
                for i in range(c.HQ // 128 + 1):
                    if i < c.HQ // 128:
                        j = jt0 + i
                        par = i % 2
                        aev = ln_tile((lambda j=j: x_res[:, j * c.DM:(j + 1) * c.DM]),
                                      (lambda par=par: xn_tm[:, par * c.DM:(par + 1) * c.DM]),
                                      dve_waits=[("sxq", sx_xq)],
                                      act_waits=rwait(rd_xn[par]))
                        cur_t = (i, par, aev)
                    else:
                        cur_t = None
                    if pend_t is not None:
                        i2, par2, aev2 = pend_t
                        dst = (lambda i2=i2: xnq_fm[:, 0:c.NCF * c.HQ]
                               .rearrange("p (ch t) -> p ch t", ch=c.NCF)[:, :, i2 * 128:(i2 + 1) * 128])
                        dvt, lpe = transpose_tile(
                            (lambda par2=par2: xn_tm[:, par2 * c.DM:(par2 + 1) * c.DM]),
                            dst, pe_waits=[("dve", aev2)], dve_waits=rwait(rd_xnq))
                        rset(rd_xn[par2], "pe", lpe)
                        last_tr_dve = dvt
                    pend_t = cur_t
                for h in range(c.NH):
                    buf = h % 2
                    lastq = None
                    for ch in range(c.NCF):
                        w = []
                        if ch == 0:
                            w = [("dve", last_tr_dve), ("sA", sA_now)] + rwait(rd_psC[buf])
                        lastq = emit("pe", (lambda ch=ch, h=h, buf=buf: (lambda t: t.matmul(
                            psC[0:64, buf * 512:buf * 512 + c.HQ],
                            arena[:, ch * c.DM + h * c.HD:ch * c.DM + (h + 1) * c.HD],
                            xnq_fm[:, ch * c.HQ:(ch + 1) * c.HQ],
                            start=(ch == 0), stop=(ch == c.NCF - 1))))(), waits=w, inc="pe")
                    st["qproj_pe"] = lastq
                    rset(rd_xnq, "pe", lastq)
                    aq = emit("act", (lambda h=h, buf=buf: (lambda s: s.activation(
                        qprime()[0:64, h * c.HQ:(h + 1) * c.HQ],
                        psC[0:64, buf * 512:buf * 512 + c.HQ], AF.Identity,
                        scale=f32(c.HD ** -0.5))))(),
                        waits=[("pe", lastq)] + rwait(rd_big), inc="act")
                    a2 = emit("act", (lambda buf=buf: (lambda s: s.activation(
                        q2sb[0:64, 0:c.HQ], psC[0:64, buf * 512:buf * 512 + c.HQ],
                        AF.Square)))(), waits=rwait(rd_q2), inc="act")
                    rset(rd_psC[buf], "act", a2)
                    dvqm = emit("dve", (lambda h=h: (lambda v: v.reduce_max(
                        q2m[0:64, h:h + 1], q2sb[0:64, 0:c.HQ], axis=AX.X)))(),
                        waits=[("act", a2)], inc="dve")
                    rset(rd_q2, "dve", dvqm)
                pb = emit("pe", (lambda: (lambda t: t.matmul(
                    psD[0:1, 0:c.NH], onesb65[:, :], q2m[0:64, 0:c.NH],
                    start=True, stop=True)))(),
                    waits=[("dve", dvqm)] + rwait(rd_psD), inc="pe")
                rset(rd_psD, "dve", emit("dve", (lambda pb=pb: (lambda v: v.reduce_max(
                    bb65[0:1, KMAXC + 1:KMAXC + 2], psD[0:1, 0:c.NH], axis=AX.X)))(),
                    waits=[("pe", pb)], inc="dve"))
                drain("dve")
                dvb = emit("dve", lambda v: v.tensor_tensor(
                    bb65[0:1, KMAXC + 2:KMAXC + 3], bb65[0:1, KMAXC + 1:KMAXC + 2],
                    bb65[0:1, KMAXC:KMAXC + 1], op=ALU.mult),
                    waits=[("dve", dv_kmax)], inc="dve")
                ab = emit("act", lambda s: s.activation(
                    bb65[0:1, KMAXC + 3:KMAXC + 4], bb65[0:1, KMAXC + 2:KMAXC + 3],
                    AF.Sqrt, scale=f32((c.HD ** -1.0) * 1.0609)), waits=[("dve", dvb)],
                    inc="act")
                pb0 = emit("pe", lambda t: t.matmul(
                    psA[0:128, 0:1], ones01[0:1, 0:128],
                    bb65[0:1, KMAXC + 3:KMAXC + 4],
                    start=True, stop=True),
                    waits=[("act", ab)] + rwait(rd_psA), inc="pe")
                dvq = emit("dve", lambda v: v.tensor_scalar(
                    nb_bc[:, 0:1], psA[0:128, 0:1], -1.0, None, op0=ALU.mult),
                    waits=[("pe", pb0)], inc="dve")
                rset(rd_psA, "dve", dvq)
                emit("sync", (lambda: (lambda sy: sy.dma_start(
                    bigbf[64:128, 0:c.NH * c.HQ], bigbf[0:64, 0:c.NH * c.HQ])))(),
                    waits=[("act", cnt["act"]), ("pe", st["qdup_rd"]), ("sr", st_kdup)],
                    inc="sr", amt=16)
                st_qdup = cnt["sr"]

                def oproj_body(j):
                    for hh in range(c.NDH):
                        lasto = None
                        for h in range(c.NH):
                            w = []
                            if h == 0:
                                w = [("dve", st["attnevac"]), ("sA", sA_now)] + rwait(rd_psD)
                            wob = c.NCF * c.DM
                            lasto = emit("pe", (lambda h=h, hh=hh, wob=wob: (lambda t: t.matmul(
                                psD[0:128, 0:c.DH],
                                attn_dm()[:, h * 128:(h + 1) * 128],
                                arena[0:64, wob + h * c.DM + hh * c.DH:
                                      wob + h * c.DM + (hh + 1) * c.DH],
                                start=(h == 0), stop=False)))(), waits=w, inc="pe")
                        lasto = emit("pe", (lambda hh=hh: (lambda t: t.matmul(
                            psD[0:128, 0:c.DH], ones1b[0:1, 0:128],
                            bo_bf[0:1, hh * c.DH:(hh + 1) * c.DH],
                            start=False, stop=True)))(), inc="pe")
                        st["oproj_pe"] = lasto
                        rset(rd_attn, "pe", lasto)
                        for m_ in range(c.NCI):
                            if m_ * c.TB * 4 < c.NH * 128 * 2:
                                rset(rd_hg[m_], "pe", lasto)
                        rset(rd_psD, "act", emit("act", (lambda hh=hh: (lambda s: s.activation(
                            oz()[:, hh * c.DH:(hh + 1) * c.DH], psD[0:128, 0:c.DH],
                            AF.Identity)))(),
                            waits=[("pe", lasto)] + rwait(rd_racc), inc="act"))
                    drain("dve")
                    aeo = ln_tile(lambda: oz()[:, :], lambda: nrm[:, :],
                                  dve_waits=[("act", cnt["act"])],
                                  act_waits=rwait(rd_nrm))
                    rset(rd_racc, "dve", aeo)
                    rset(rd_racc, "act", 0)
                    drain("dve")
                    xres_add[j] = emit("dve", (lambda j=j: (lambda v: v.tensor_tensor(
                        x_res[:, j * c.DM:(j + 1) * c.DM], x_res[:, j * c.DM:(j + 1) * c.DM],
                        nrm[:, :], op=ALU.add)))(), inc="dve")
                    rset(rd_nrm, "dve", xres_add[j])

                for i in range(c.HQ // 128):
                    j = jt0 + i
                    gt = QT0 + j
                    for pp in range(c.NPASS):
                        avp = st["avp"] = st.get("avp", 0) ^ 1
                        pend = []
                        avlast = None

                        def emit_scores(kt, pp=pp, i=i):
                            buf = kt % 2
                            ps_sc = psA if buf == 0 else psB
                            rd_sc = rd_psA if buf == 0 else rd_psB
                            rp = 64 * (kt % 2)
                            lasts = None
                            for mmi in range(c.NMM):
                                h0 = pp * c.HP + mmi * c.HM
                                w = []
                                if mmi == 0:
                                    w = [("dve", dvq), ("sr", st_qdup)] + rwait(rd_sc)
                                lasts = emit("pe", (lambda kt=kt, h0=h0, mmi=mmi, rp=rp,
                                                    ps_sc=ps_sc, i=i: (lambda t: t.matmul(
                                    ps_sc[0:128, mmi * c.HM * 128:(mmi + 1) * c.HM * 128],
                                    kprime[rp:rp + 64, kt * 128:(kt + 1) * 128],
                                    bigbf[0:128, 0:c.NH * c.HQ]
                                    .rearrange("p (h t) -> p h t", h=c.NH)
                                    [rp:rp + 64, h0:h0 + c.HM, i * 128:(i + 1) * 128],
                                    start=True, stop=True,
                                    tile_position=(rp, 0) if rp else None)))(),
                                    waits=w, inc="pe")
                            if rp:
                                st["qdup_rd"] = lasts
                            rset(rd_big, "pe", lasts)
                            ax = emit("act", (lambda buf=buf, ps_sc=ps_sc: (lambda s: s.activation(
                                hp_ar[:, buf * c.HP * 128:(buf + 1) * c.HP * 128],
                                ps_sc[0:128, 0:c.HP * 128], AF.Exp,
                                bias=nb_bc[:, 0:1])))(),
                                waits=[("pe", lasts)] + rwait(rd_P[buf]), inc="act")
                            rset(rd_sc, "act", ax)
                            pwait = ("act", ax)
                            if kt == gt:
                                dvm = emit("dve", (lambda buf=buf: (lambda v: v.tensor_tensor(
                                    hp_ar[:, buf * c.HP * 128:(buf + 1) * c.HP * 128],
                                    hp_ar[:, buf * c.HP * 128:(buf + 1) * c.HP * 128],
                                    mask_rep[:, 0:c.HP * 128], op=ALU.mult)))(),
                                    waits=[("act", ax)], inc="dve")
                                pwait = ("dve", dvm)
                            return (buf, pwait, kt)

                        def emit_av(ent, gt=gt):
                            nonlocal avlast
                            pbuf, pwait, pkt = ent
                            avl = None
                            for g2 in range(c.NMM):
                                w = [pwait]
                                if pkt == 0 and g2 == 0:
                                    w += rwait(rd_psC[0], rd_psC[1])
                                avl = emit("pe", (lambda pkt=pkt, g2=g2, pbuf=pbuf, gt=gt:
                                                  (lambda t: t.matmul(
                                    psC[0:65, g2 * c.HM * 128:(g2 + 1) * c.HM * 128],
                                    vprime[:, pkt * 66:pkt * 66 + 65],
                                    hp_ar[:, pbuf * c.HP * 128 + g2 * c.HM * 128:
                                           pbuf * c.HP * 128 + (g2 + 1) * c.HM * 128],
                                    start=(pkt == 0), stop=(pkt == gt),
                                    skip_group_check=True)))(), waits=w, inc="pe")
                            rset(rd_P[pbuf], "pe", avl)
                            avlast = avl

                        for k0 in range(0, gt + 1, 2):
                            for ent in pend:
                                emit_av(ent)
                            pend = [emit_scores(k0)]
                            if k0 + 1 <= gt:
                                pend.append(emit_scores(k0 + 1))
                            if k0 == min(4, 2 * (gt // 2)):
                                flush_fin()
                        for ent in pend:
                            emit_av(ent)
                        dvav = emit("dve", (lambda avp=avp: (lambda v: v.tensor_copy(
                            av_scr[0:65, avp * c.HP * 128:(avp + 1) * c.HP * 128],
                            psC[0:65, 0:c.HP * 128])))(),
                            waits=[("pe", avlast)] + rwait(rd_avs[avp]), inc="dve")
                        rset(rd_psC[0], "dve", dvav)
                        rset(rd_psC[1], "dve", dvav)

                        def fin_pass(pp=pp, avp=avp):
                            drain("dve")
                            dvr = emit("dve", (lambda avp=avp: (lambda v: v.reciprocal(
                                r65[64:65, 0:c.HP * 128],
                                av_scr[64:65, avp * c.HP * 128:(avp + 1) * c.HP * 128])))(),
                                waits=[("pe", st["r_pe"])], inc="dve", lp=True)
                            drain("dve")
                            for rj in range(c.NMM):
                                lo = rj * c.HM * 128
                                prb = emit("pe", (lambda rj=rj, lo=lo: (lambda t: t.matmul(
                                    psD[0:64, 0:c.HM * 128], ones65r[64:65, 0:64],
                                    r65[64:65, lo:lo + c.HM * 128],
                                    start=True, stop=True, tile_position=(64, 0))))(),
                                    waits=[("dve", dvr)] + rwait(rd_psD), inc="pe")
                                st["r_pe"] = prb
                                rset(rd_psD, "dve", emit("dve", (lambda lo=lo, prb=prb:
                                                                 (lambda v: v.tensor_copy(
                                    rb_sb[0:64, lo:lo + c.HM * 128],
                                    psD[0:64, 0:c.HM * 128])))(),
                                    waits=[("pe", prb)], inc="dve"))
                            drain("dve")
                            dve_ae = emit("dve", (lambda pp=pp, avp=avp: (lambda v: v.tensor_tensor(
                                attn_dm()[:, pp * c.HP * 128:(pp + 1) * c.HP * 128],
                                av_scr[0:64, avp * c.HP * 128:(avp + 1) * c.HP * 128],
                                rb_sb[:, :], op=ALU.mult)))(),
                                waits=[("pe", st["r_pe"])] + rwait(rd_attn), inc="dve")
                            rset(rd_avs[avp], "dve", dve_ae)
                            st["attnevac"] = dve_ae
                        st["fin"].append(fin_pass)
                    def fin_qtile(j=j):
                        oproj_body(j)
                    st["fin"].append(fin_qtile)

            flush_fin()
            lastflash_pe = cnt["pe"]

            pend_t = None
            for j in range(c.NTQ + 1):
                if j < c.NTQ:
                    par = j % 2
                    drain("dve")
                    aev = ln_tile((lambda j=j: x_res[:, j * c.DM:(j + 1) * c.DM]),
                                  (lambda par=par: xn_tm[:, par * c.DM:(par + 1) * c.DM]),
                                  dve_waits=[("dve", xres_add[j])],
                                  act_waits=rwait(rd_xn[par]))
                    cur_t = (j, par, aev)
                else:
                    cur_t = None
                if pend_t is None:
                    pend_t = cur_t
                    continue
                j, par, aev = pend_t
                pend_t = cur_t
                dst = (lambda j=j: bigbf[0:128, 0:c.NCF * c.QUARTER]
                       .rearrange("p (ch t) -> p ch t", ch=c.NCF)[:, :, j * 128:(j + 1) * 128])
                dvt, lpe = transpose_tile(
                    (lambda par=par: xn_tm[:, par * c.DM:(par + 1) * c.DM]),
                    dst, pe_waits=[("dve", aev), ("pe", lastflash_pe)],
                    dve_waits=rwait(rd_big))
                rset(rd_xn[par], "pe", lpe)
                xn2_dve[j] = dvt
                lastg = None
                for ch in range(c.NCF):
                    w = []
                    if ch == 0:
                        w = [("dve", dvt)] + rwait(rd_psD)
                    lastg = emit("pe", (lambda ch=ch, j=j: (lambda t: t.matmul(
                        psD[0:128, 0:c.E],
                        bigbf[0:128, ch * c.QUARTER + j * 128:ch * c.QUARTER + (j + 1) * 128],
                        gate_sb[:, ch * c.E:(ch + 1) * c.E],
                        start=(ch == 0), stop=(ch == c.NCF - 1))))(), waits=w, inc="pe")
                rset(rd_big, "pe", lastg)
                sig = emit("act", (lambda j=j: (lambda s: s.activation(
                    sc_r[:, j * c.E:(j + 1) * c.E], psD[0:128, 0:c.E], AF.Sigmoid)))(),
                    waits=[("pe", lastg), ("dve", st["comb_done"])], inc="act")
                rset(rd_psD, "act", sig)
                st["sig_%d" % j] = sig
            T_ = c.NTQ

            def rstep(fn, waits=()):
                for j in range(T_):
                    emit("dve", (lambda j=j: (lambda v: fn(v, j)))(),
                         waits=(waits if j == 0 else ()))
                drain("dve")

            for j in range(T_):
                emit("dve", (lambda j=j: (lambda v: v.tensor_tensor(
                    sc_r[:, j * c.E:(j + 1) * c.E], sc_r[:, j * c.E:(j + 1) * c.E],
                    bg_bc[:, :], op=ALU.add)))(),
                    waits=[("act", st["sig_%d" % j])])
            drain("dve")
            for g2 in range(c.G):
                rstep(lambda v, j, g2=g2: v.reduce_max(
                    gs_r[:, j * c.G + g2:j * c.G + g2 + 1],
                    sc_r[:, j * c.E + g2 * c.EPG:j * c.E + (g2 + 1) * c.EPG], axis=AX.X))
            rstep(lambda v, j: v.tensor_copy(gg_r[:, j * c.G:(j + 1) * c.G],
                                             gs_r[:, j * c.G:(j + 1) * c.G]))
            for it in range(c.TG):
                rstep(lambda v, j: v.reduce_max(mcol[:, j:j + 1],
                                                gg_r[:, j * c.G:(j + 1) * c.G], axis=AX.X))
                if it < c.TG - 1:
                    rstep(lambda v, j: v.tensor_scalar(
                        gw_r[:, j * c.G:(j + 1) * c.G], gg_r[:, j * c.G:(j + 1) * c.G],
                        mcol[:, j:j + 1], None, op0=ALU.is_equal))
                    rstep(lambda v, j: v.scalar_tensor_tensor(
                        gg_r[:, j * c.G:(j + 1) * c.G], gw_r[:, j * c.G:(j + 1) * c.G],
                        -1e30, gg_r[:, j * c.G:(j + 1) * c.G], op0=ALU.mult, op1=ALU.add))
            rstep(lambda v, j: v.tensor_scalar(
                gs_r[:, j * c.G:(j + 1) * c.G], gs_r[:, j * c.G:(j + 1) * c.G],
                mcol[:, j:j + 1], None, op0=ALU.is_ge))
            for g2 in range(c.G):
                rstep(lambda v, j, g2=g2: v.tensor_scalar(
                    gw_r[:, j * c.E + g2 * c.EPG:j * c.E + (g2 + 1) * c.EPG],
                    ones_r[:, 0:c.EPG], gs_r[:, j * c.G + g2:j * c.G + g2 + 1], None,
                    op0=ALU.mult))
            rstep(lambda v, j: v.tensor_tensor(
                gw_r[:, j * c.E:(j + 1) * c.E], gw_r[:, j * c.E:(j + 1) * c.E],
                sc_r[:, j * c.E:(j + 1) * c.E], op=ALU.mult))
            rstep(lambda v, j: v.tensor_copy(comb[:, j * c.E:(j + 1) * c.E],
                                             gw_r[:, j * c.E:(j + 1) * c.E]),
                  waits=rwait(rd_racc))
            for it in range(c.TK):
                rstep(lambda v, j: v.reduce_max(mcol[:, j:j + 1],
                                                comb[:, j * c.E:(j + 1) * c.E], axis=AX.X))
                if it < c.TK - 1:
                    rstep(lambda v, j: v.tensor_scalar(
                        sc_r[:, j * c.E:(j + 1) * c.E], comb[:, j * c.E:(j + 1) * c.E],
                        mcol[:, j:j + 1], None, op0=ALU.is_equal))
                    rstep(lambda v, j: v.scalar_tensor_tensor(
                        comb[:, j * c.E:(j + 1) * c.E], sc_r[:, j * c.E:(j + 1) * c.E],
                        -1e30, comb[:, j * c.E:(j + 1) * c.E], op0=ALU.mult, op1=ALU.add))
            rstep(lambda v, j: v.tensor_scalar(
                sc_r[:, j * c.E:(j + 1) * c.E], gw_r[:, j * c.E:(j + 1) * c.E],
                mcol[:, j:j + 1], None, op0=ALU.is_ge))
            rstep(lambda v, j: v.tensor_tensor(
                comb[:, j * c.E:(j + 1) * c.E], gw_r[:, j * c.E:(j + 1) * c.E],
                sc_r[:, j * c.E:(j + 1) * c.E], op=ALU.mult))
            rstep(lambda v, j: v.reduce_sum(mcol[:, j:j + 1],
                                            comb[:, j * c.E:(j + 1) * c.E], axis=AX.X))
            rstep(lambda v, j: v.tensor_scalar(mcol[:, j:j + 1], mcol[:, j:j + 1],
                                               1e-8, None, op0=ALU.add))
            rstep(lambda v, j: v.reciprocal(mcol[:, j:j + 1], mcol[:, j:j + 1]))
            rstep(lambda v, j: v.tensor_scalar(
                comb[:, j * c.E:(j + 1) * c.E], comb[:, j * c.E:(j + 1) * c.E],
                mcol[:, j:j + 1], None, op0=ALU.mult))
            st["comb_done"] = cnt["dve"]

            dn_pe = {}
            evac_dve = {}
            silu_a = {}
            hp_dve = {}
            gch, uch, dnch = [0], [0], [0]
            hp_rd_pe = {(tp_, m): lastflash_pe for tp_ in range(2)
                        for m in range(c.NCI)}
            last_hp = {}

            for se in range(c.NSLOT):
                p = se % 2
                sE = "sE" + str(p)
                if se >= 2:
                    ww = [("pe", dn_pe[se - 2])]
                else:
                    ww = [("pe", st["oproj_pe"]), ("pe", st["qproj_pe"])]
                if se < 2:
                    srcs = [wsg_d[:, se * c.INTER:(se + 1) * c.INTER],
                            wsu_d[:, se * c.INTER:(se + 1) * c.INTER],
                            wsd_d[se * c.INTER:(se + 1) * c.INTER, :]]
                else:
                    srcs = [weg_d[se - 2], weu_d[se - 2], wed_d[se - 2]]
                for mat in range(3):
                    src = srcs[mat]
                    nch = c.NCF if mat < 2 else c.NCI
                    emit("pool", (lambda p=p, mat=mat, src=src, nch=nch: (lambda g: g.dma_start(
                        eslot(p * 3 + mat).rearrange("p (ch i) -> p ch i", ch=nch),
                        src.rearrange("(ch p) i -> p ch i", p=128))))(),
                        waits=(ww if mat == 0 else ()), inc=sE, amt=16)
                st["eloads"][p] += 1
                sE_now = cnt[sE]

                for tb in range(c.NTB):
                    for mat in range(2):
                        pgu = psA if mat == 0 else psB
                        rd_gu = rd_psA if mat == 0 else rd_psB
                        cidx = gch if mat == 0 else uch
                        for m in range(c.NCI):
                            half = m % 2
                            w = []
                            if m == 0:
                                w = [(sE, sE_now)]
                                if se < 2 and tb == 0:
                                    w.append(("dve", xn2_dve[c.NTQ - 1]))
                            w = w + (rwait(rd_gu) if (cidx[0] % 2 == 0 or cidx[0] < 2) else [])
                            prev = cidx[0] - 2
                            if prev >= 0:
                                if mat == 0:
                                    w.append(("act", silu_a[prev]))
                                else:
                                    w.append(("dve", hp_dve[prev]))
                            else:
                                w += rwait(rd_gu)
                            lastm = None
                            for ch in range(c.NCF):
                                lastm = emit("pe", (lambda ch=ch, m=m, half=half, tb=tb,
                                                    p=p, mat=mat, pgu=pgu: (lambda t: t.matmul(
                                    pgu[0:128, half * 512:half * 512 + c.TB],
                                    eslot(p * 3 + mat)[:, ch * c.INTER + m * 128:
                                                       ch * c.INTER + (m + 1) * 128],
                                    bigbf[0:128, ch * c.QUARTER + tb * c.TB:
                                          ch * c.QUARTER + (tb + 1) * c.TB],
                                    start=(ch == 0), stop=(ch == c.NCF - 1))))(),
                                    waits=(w if ch == 0 else ()), inc="pe")
                            rset(rd_big, "pe", lastm)
                            if mat == 0:
                                silu_a[cidx[0]] = emit("act", (lambda m=m, half=half:
                                                               (lambda s: s.activation(
                                    hg[:, m * c.TB:(m + 1) * c.TB],
                                    psA[0:128, half * 512:half * 512 + c.TB], AF.Silu)))(),
                                    waits=[("pe", lastm)] + rwait(rd_hg[m]) +
                                          (rwait(rd_attn) if m * c.TB * 4 < c.NH * 128 * 2 else []),
                                    inc="act")
                                rset(rd_psA, "act", silu_a[cidx[0]])
                            else:
                                tpar = tb % 2
                                hp_dve[cidx[0]] = emit("dve", (lambda m=m, half=half, tpar=tpar:
                                                               (lambda v: v.tensor_tensor(
                                    hp_ar[:, tpar * c.NCI * c.TB + m * c.TB:
                                           tpar * c.NCI * c.TB + (m + 1) * c.TB],
                                    hg[:, m * c.TB:(m + 1) * c.TB],
                                    psB[0:128, half * 512:half * 512 + c.TB],
                                    op=ALU.mult)))(),
                                    waits=[("pe", lastm), ("act", silu_a[cidx[0]]),
                                           ("pe", hp_rd_pe[(tb % 2, m)])],
                                    inc="dve")
                                last_hp[(tb % 2, m)] = hp_dve[cidx[0]]
                                rset(rd_psB, "dve", hp_dve[cidx[0]])
                                rset(rd_hg[m], "dve", hp_dve[cidx[0]])
                            cidx[0] += 1
                    hp_base = uch[0] - c.NCI
                    for tt in range(c.NTT):
                        for hh in range(c.NDH):
                            slot4 = dnch[0] % 4
                            if slot4 == 0:
                                dn_ps, dn_rd = psC, rd_psC[0]
                                dn_lo = 0
                            elif slot4 == 1:
                                dn_ps, dn_rd = psC, rd_psC[1]
                                dn_lo = 512
                            elif slot4 == 2:
                                dn_ps, dn_rd = psD, rd_psD
                                dn_lo = 0
                            else:
                                dn_ps, dn_rd = psT, rd_psT
                                dn_lo = 0
                            w = []
                            prev = dnch[0] - 4
                            if prev >= 0:
                                w.append(("dve", evac_dve[prev]))
                            else:
                                w += rwait(dn_rd)
                            lastd = None
                            tpar = tb % 2
                            for ic in range(c.NCI):
                                wic = list(w) if ic == 0 else []
                                wic.append(("dve", hp_dve[hp_base + ic]))
                                lastd = emit("pe", (lambda ic=ic, tt=tt, hh=hh, dn_ps=dn_ps,
                                                    dn_lo=dn_lo, tpar=tpar, p=p: (lambda t: t.matmul(
                                    dn_ps[0:128, dn_lo:dn_lo + c.DH],
                                    hp_ar[:, tpar * c.NCI * c.TB + ic * c.TB + tt * 128:
                                           tpar * c.NCI * c.TB + ic * c.TB + (tt + 1) * 128],
                                    eslot(p * 3 + 2)[:, ic * c.DM + hh * c.DH:
                                                     ic * c.DM + (hh + 1) * c.DH],
                                    start=(ic == 0), stop=(ic == c.NCI - 1))))(),
                                    waits=wic, inc="pe")
                            dn_pe[se] = lastd
                            tile = tb * c.NTT + tt
                            roff = tile * c.DM + hh * c.DH
                            ew = [("pe", lastd)]
                            if se == 0:
                                ew += rwait(rd_racc)
                            if tt == 0 and hh == 0:
                                drain("dve")
                            if se == 0:
                                evac_dve[dnch[0]] = emit("dve", (lambda roff=roff, dn_ps=dn_ps,
                                                                 dn_lo=dn_lo:
                                                                 (lambda v: v.tensor_copy(
                                    racc[:, roff:roff + c.DH],
                                    dn_ps[0:128, dn_lo:dn_lo + c.DH])))(),
                                    waits=ew, inc="dve")
                            elif se == 1:
                                evac_dve[dnch[0]] = emit("dve", (lambda roff=roff, dn_ps=dn_ps,
                                                                 dn_lo=dn_lo:
                                                                 (lambda v: v.tensor_tensor(
                                    racc[:, roff:roff + c.DH],
                                    dn_ps[0:128, dn_lo:dn_lo + c.DH],
                                    racc[:, roff:roff + c.DH], op=ALU.add)))(),
                                    waits=ew, inc="dve")
                            else:
                                ccol = tile * c.E + (se - 2)
                                evac_dve[dnch[0]] = emit("dve", (lambda roff=roff, dn_ps=dn_ps,
                                                                 dn_lo=dn_lo, ccol=ccol: (lambda v:
                                                                 v.scalar_tensor_tensor(
                                    racc[:, roff:roff + c.DH],
                                    dn_ps[0:128, dn_lo:dn_lo + c.DH],
                                    comb[:, ccol:ccol + 1],
                                    racc[:, roff:roff + c.DH],
                                    op0=ALU.mult, op1=ALU.add)))(),
                                    waits=ew, inc="dve")
                            rset(dn_rd, "dve", evac_dve[dnch[0]])
                            dnch[0] += 1
                        for m_ in range(c.NCI):
                            hp_rd_pe[(tpar, m_)] = dn_pe[se]


            st["expert_pe"] = dn_pe[c.NSLOT - 1]
            rset(rd_P[0], "pe", st["expert_pe"])
            rset(rd_P[1], "pe", st["expert_pe"])
            last_evac = evac_dve[dnch[0] - 1]

            for j in range(c.NTQ):
                par = j % 2
                drain("dve")
                aev = ln_tile((lambda j=j: racc[:, j * c.DM:(j + 1) * c.DM]),
                              (lambda: nrm[:, :]),
                              dve_waits=[("dve", last_evac)],
                              act_waits=rwait(rd_nrm))
                rset(rd_racc, "dve", aev)
                drain("dve")
                dvo = emit("dve", (lambda j=j, par=par: (lambda v: v.tensor_tensor(
                    out_sb[:, par * c.DM:(par + 1) * c.DM],
                    x_res[:, j * c.DM:(j + 1) * c.DM], nrm[:, :], op=ALU.add)))(),
                    waits=[("so" + str(par), st["out_so"][par])], inc="dve")
                rset(rd_nrm, "dve", dvo)
                rset(rd_racc, "dve", dvo)
                gt = QT0 + j
                emit("sync", (lambda gt=gt, par=par: (lambda sy: sy.dma_start(
                    out_d[gt * 128:(gt + 1) * 128, :],
                    out_sb[:, par * c.DM:(par + 1) * c.DM])))(),
                    waits=[("dve", dvo)], inc="so" + str(par), amt=16)
                st["out_so"][par] = cnt["so" + str(par)]
            st["xres_free"] = cnt["dve"]

        emit("sync", lambda sy: sy.nop(),
             waits=[("so0", cnt["so0"]), ("so1", cnt["so1"])])
        return finish()

    return nc


def _layout(inputs, c: Cfg):
    ac = np.ascontiguousarray
    return {
        "wq": ac(inputs["L0_attn_Wq"]),
        "wk": ac(inputs["L0_attn_Wk"]),
        "wv": ac(inputs["L0_attn_Wv"]),
        "wo": ac(inputs["L0_attn_Wo"]),
        "bo": ac(inputs["L0_attn_bo"]).reshape(1, c.DM),
        "wgate": ac(inputs["L0_ffn_gate_w"].T),
        "bgate": ac(inputs["L0_ffn_gate_bias"]).reshape(1, c.E),
        "weg": ac(inputs["L0_ffn_eg_w"].transpose(0, 2, 1)),
        "weu": ac(inputs["L0_ffn_eu_w"].transpose(0, 2, 1)),
        "wed": ac(inputs["L0_ffn_ed_w"].transpose(0, 2, 1)),
        "wsg": ac(inputs["L0_ffn_sg_w"].T),
        "wsu": ac(inputs["L0_ffn_su_w"].T),
        "wsd": ac(inputs["L0_ffn_sd_w"].T),
    }


def kernel(**inputs):
    c = FULL
    x = np.asarray(inputs["L0_attn_x"], dtype=np.float32)
    B = x.shape[0]
    shared = _layout({k: np.asarray(v, dtype=np.float32) for k, v in inputs.items()}, c)
    nc = build(c)
    in_maps = []
    for b in range(B):
        m = dict(shared)
        m["x"] = np.ascontiguousarray(x[b])
        in_maps.append(m)
    res = run_bass_kernel_spmd(nc, in_maps, core_ids=list(range(B)))
    out = np.stack([np.asarray(r["out"]) for r in res.results], axis=0)
    return out.astype(np.float32)


# version 18
# speedup vs baseline: 1.0930x; 1.0930x over previous
import contextlib
import numpy as np
import concourse.bass as bass
import concourse.mybir as mybir
from concourse.bass_utils import run_bass_kernel_spmd

F32 = mybir.dt.float32
BF16 = mybir.dt.bfloat16
I32 = mybir.dt.int32
AF = mybir.ActivationFunctionType
ALU = mybir.AluOpType
AX = mybir.AxisListType

f32 = np.float32


class Cfg:
    def __init__(self, SEQ=4096, DM=1024, NH=16, E=64, G=8, TG=4, TK=6,
                 INTER=512, NQUARTER=4):
        self.SEQ, self.DM, self.NH, self.HD = SEQ, DM, NH, 64
        self.E, self.G, self.TG, self.TK = E, G, TG, TK
        self.INTER, self.ISH = INTER, 2 * INTER
        self.NQUARTER = NQUARTER
        self.QUARTER = SEQ // NQUARTER
        assert SEQ % NQUARTER == 0 and self.QUARTER % 128 == 0
        self.NTQ = self.QUARTER // 128
        self.NTILES = SEQ // 128
        self.NCF = DM // 128
        self.NCI = INTER // 128
        assert NH * self.HD == DM and DM % 128 == 0 and INTER % 128 == 0
        self.HM = min(4, NH)
        self.HP = min(8, NH)
        assert NH % self.HP == 0 and self.HP % self.HM == 0
        self.NPASS = NH // self.HP
        self.NMM = self.HP // self.HM
        self.HQ = min(512, self.QUARTER)
        self.NHQ = self.QUARTER // self.HQ
        self.TB = min(512, self.QUARTER)
        self.NTB = self.QUARTER // self.TB
        self.NTT = self.TB // 128
        self.DH = min(512, DM)
        self.NDH = DM // self.DH
        self.SLOT = self.NCF * INTER
        assert self.SLOT == self.NCI * DM
        self.EPG = E // G
        self.NSLOT = self.E + 2


FULL = Cfg()


def build(c: Cfg):
    nc = bass.Bass("TRN2", target_bir_lowering=False)

    x_d = nc.dram_tensor("x", [c.SEQ, c.DM], F32, kind="ExternalInput").ap()
    wq_d = nc.dram_tensor("wq", [c.DM, c.DM], F32, kind="ExternalInput").ap()
    wk_d = nc.dram_tensor("wk", [c.DM, c.HD], F32, kind="ExternalInput").ap()
    wv_d = nc.dram_tensor("wv", [c.DM, c.HD], F32, kind="ExternalInput").ap()
    wo_d = nc.dram_tensor("wo", [c.DM, c.DM], F32, kind="ExternalInput").ap()
    bo_d = nc.dram_tensor("bo", [1, c.DM], F32, kind="ExternalInput").ap()
    wgate_d = nc.dram_tensor("wgate", [c.DM, c.E], F32, kind="ExternalInput").ap()
    bgate_d = nc.dram_tensor("bgate", [1, c.E], F32, kind="ExternalInput").ap()
    weg_d = nc.dram_tensor("weg", [c.E, c.DM, c.INTER], F32, kind="ExternalInput").ap()
    weu_d = nc.dram_tensor("weu", [c.E, c.DM, c.INTER], F32, kind="ExternalInput").ap()
    wed_d = nc.dram_tensor("wed", [c.E, c.INTER, c.DM], F32, kind="ExternalInput").ap()
    wsg_d = nc.dram_tensor("wsg", [c.DM, c.ISH], F32, kind="ExternalInput").ap()
    wsu_d = nc.dram_tensor("wsu", [c.DM, c.ISH], F32, kind="ExternalInput").ap()
    wsd_d = nc.dram_tensor("wsd", [c.ISH, c.DM], F32, kind="ExternalInput").ap()
    out_d = nc.dram_tensor("out", [c.SEQ, c.DM], F32, kind="ExternalOutput").ap()

    with contextlib.ExitStack() as ctx:
        e_ = ctx.enter_context

        x_res = e_(nc.sbuf_tensor("x_res", [128, c.NTQ * c.DM], F32))
        kprime = e_(nc.sbuf_tensor("kprime", [128, c.SEQ], BF16))
        vprime = e_(nc.sbuf_tensor("vprime", [128, c.NTILES * 66], BF16))
        racc = e_(nc.sbuf_tensor("racc", [128, c.NTQ * c.DM], F32))
        arena = e_(nc.sbuf_tensor("arena", [128, 6 * c.SLOT], BF16))
        nbig = max(c.NH * c.HQ, c.NCF * c.QUARTER)
        bigbf = e_(nc.sbuf_tensor("bigbf", [128, nbig], BF16))
        nhp = max(2 * c.HP * 128, 2 * c.NCI * c.TB)
        hp_ar = e_(nc.sbuf_tensor("hp_ar", [128, nhp], BF16))
        hg = e_(nc.sbuf_tensor("hg", [128, max(c.NCI * c.TB, c.NH * 64)], F32))
        xnq_fm = e_(nc.sbuf_tensor("xnq_fm", [128, max(c.NCF * c.HQ, 2 * c.NCF * 128)], BF16))
        xn_tm = e_(nc.sbuf_tensor("xn_tm", [128, 2 * c.DM], BF16))
        out_sb = e_(nc.sbuf_tensor("out_sb", [128, 2 * c.DM], F32))
        nrm = e_(nc.sbuf_tensor("nrm", [128, c.DM], F32))
        comb = e_(nc.sbuf_tensor("comb", [128, c.NTQ * c.E], F32))
        sc_r = e_(nc.sbuf_tensor("sc_r", [128, c.NTQ * c.E], F32))
        gw_r = e_(nc.sbuf_tensor("gw_r", [128, c.NTQ * c.E], F32))
        gs_r = e_(nc.sbuf_tensor("gs_r", [128, c.NTQ * c.G], F32))
        gg_r = e_(nc.sbuf_tensor("gg_r", [128, c.NTQ * c.G], F32))
        mcol = e_(nc.sbuf_tensor("mcol", [128, c.NTQ], F32))
        rb_sb = e_(nc.sbuf_tensor("rb_sb", [64, c.HP * 128], F32))
        av_scr = e_(nc.sbuf_tensor("av_scr", [65, 2 * c.HP * 128], BF16))
        r65 = e_(nc.sbuf_tensor("r65", [65, c.HP * 128], BF16))
        ones65r = e_(nc.sbuf_tensor("ones65r", [65, 64], BF16))
        q2m = e_(nc.sbuf_tensor("q2m", [64, c.NH], BF16))
        onecol_b = e_(nc.sbuf_tensor("onecol_b", [128, 1], BF16))
        nb_bc = e_(nc.sbuf_tensor("nb_bc", [128, 1], F32))
        q2sb = e_(nc.sbuf_tensor("q2sb", [64, max(c.HQ, 128)], BF16))
        bb65 = e_(nc.sbuf_tensor("bb65", [1, max(c.NH, c.NTILES) + 4], F32))
        mask_rep = e_(nc.sbuf_tensor("mask_rep", [128, c.HP * 128], BF16))
        iota_sb = e_(nc.sbuf_tensor("iota_sb", [128, 128], I32))
        ident_bf = e_(nc.sbuf_tensor("ident_bf", [128, 128], BF16))
        wk_sb = e_(nc.sbuf_tensor("wk_sb", [128, c.NCF * c.HD], BF16))
        wv_sb = e_(nc.sbuf_tensor("wv_sb", [128, c.NCF * c.HD], BF16))
        gate_sb = e_(nc.sbuf_tensor("gate_sb", [128, c.NCF * c.E], BF16))
        bg_bc = e_(nc.sbuf_tensor("bg_bc", [128, c.E], F32))
        ones_r = e_(nc.sbuf_tensor("ones_r", [128, c.EPG], F32))
        ones01 = e_(nc.sbuf_tensor("ones01", [1, 128], F32))
        ones1b = e_(nc.sbuf_tensor("ones1b", [1, 128], BF16))
        bo_bf = e_(nc.sbuf_tensor("bo_bf", [1, c.DM], BF16))
        onesb65 = e_(nc.sbuf_tensor("onesb65", [64, 1], BF16))
        stats = e_(nc.sbuf_tensor("stats", [128, 4 * 12], F32))
        mv = e_(nc.sbuf_tensor("mv", [128, 4 * 2], F32))
        rstd = e_(nc.sbuf_tensor("rstd", [128, 4], F32))

        psA = e_(nc.psum_tensor("psA", [128, 1024], F32))
        psB = e_(nc.psum_tensor("psB", [128, 1024], F32))
        psC = e_(nc.psum_tensor("psC", [128, 1024], F32))
        psD = e_(nc.psum_tensor("psD", [128, 512], F32))
        psT = e_(nc.psum_tensor("psT", [128, 512], F32))

        sem_names = ["pe", "act", "dve", "pool",
                     "sx0", "sx1", "sxq", "srow", "sr", "so0", "so1",
                     "swinit", "sA", "sE0", "sE1"]
        sems = {n: e_(nc.semaphore(n)) for n in sem_names}
        cnt = {n: 0 for n in sem_names}
        prog = {n: [] for n in ["pe", "act", "dve", "pool", "sync"]}
        last_wait = {}

        def emit(en, fn, waits=(), inc=None, amt=1, lp=False):
            ws = []
            for (s, v) in waits:
                if v <= 0:
                    continue
                key = (en, s)
                if v > last_wait.get(key, -1):
                    ws.append((s, v))
                    last_wait[key] = v
            prog[en].append((ws, (fn, lp) if lp else fn, inc, amt))
            if inc is not None:
                cnt[inc] += amt
                return cnt[inc]
            return None

        def drain(en):
            prog[en].append(((), "drain", None, 0))

        def finish():
            with nc.Block() as block:
                def mk(en_name):
                    def body(eng):
                        for (ws, fn, inc, amt) in prog[en_name]:
                            for (s_, v_) in ws:
                                eng.wait_ge(sems[s_], v_)
                            if fn == "drain":
                                eng.drain()
                                continue
                            if isinstance(fn, tuple):
                                with nc.allow_low_precision(reason="bf16 softmax denominator (l already bf16)"):
                                    inst = fn[0](eng)
                            else:
                                inst = fn(eng)
                            if inc is not None:
                                inst.then_inc(sems[inc], amt)
                    return body
                block.tensor(mk("pe"))
                block.scalar(mk("act"))
                block.vector(mk("dve"))
                block.gpsimd(mk("pool"))
                block.sync(mk("sync"))
            return nc

        def RD():
            return {"pe": 0, "act": 0, "dve": 0}

        def oz():
            return racc[:, 0:c.DM]

        def psTb():
            return psT[0:128, 0:512].bitcast(BF16)

        def bg_row():
            return rb_sb[0:1, 0:c.E]

        def attn_dm():
            return hg[0:64, 0:c.NH * 64].bitcast(BF16)

        def rwait(*trs):
            w = []
            for tr in trs:
                for k, v in tr.items():
                    if v:
                        w.append((k, v))
            return w

        def rset(tr, eng, v):
            if v is not None and v > tr.get(eng, 0):
                tr[eng] = v

        rd_psA = RD()
        rd_psB = RD()
        rd_psC = [RD(), RD()]
        rd_psD = RD()
        rd_psT = RD()
        rd_P = [RD(), RD(), RD(), RD()]
        rd_xn = [RD(), RD()]
        rd_xnq = RD()
        rd_big = RD()
        rd_nrm = RD()
        rd_hg = [RD() for _ in range(max(c.NCI, 1))]
        rd_attn = RD()
        rd_avs = [RD(), RD()]
        rd_rb = RD()
        rd_racc = RD()
        rd_q2 = RD()

        ln_par = [0]

        def ln_tile(src_fn, dst_fn, dve_waits=(), act_waits=()):
            par = ln_par[0]
            ln_par[0] = (ln_par[0] + 1) % 4
            nhalf = (c.DM + 511) // 512
            for i in range(nhalf):
                lo, hi = i * 512, min((i + 1) * 512, c.DM)
                emit("dve", (lambda i=i, lo=lo, hi=hi, par=par:
                             (lambda v: v.bn_stats(stats[:, par * 12 + i * 6:par * 12 + (i + 1) * 6],
                                                   src_fn()[:, lo:hi])))(),
                     waits=(dve_waits if i == 0 else ()))
            drain("dve")
            dv = emit("dve", (lambda par=par, nhalf=nhalf:
                              (lambda v: v.bn_aggr(mv[:, par * 2:par * 2 + 2],
                                                   stats[:, par * 12:par * 12 + 6 * nhalf])))(),
                      inc="dve")
            av = emit("act", (lambda par=par: (lambda s: s.activation(
                rstd[:, par:par + 1], mv[:, par * 2 + 1:par * 2 + 2], AF.Sqrt,
                bias=f32(1e-5))))(), waits=[("dve", dv)], inc="act")
            emit("dve", (lambda par=par: (lambda v: v.reciprocal(
                rstd[:, par:par + 1], rstd[:, par:par + 1])))(), waits=[("act", av)])
            drain("dve")
            aev = emit("dve", (lambda par=par: (lambda v: v.tensor_scalar(
                dst_fn(), src_fn(), mv[:, par * 2:par * 2 + 1],
                rstd[:, par:par + 1], op0=ALU.subtract, op1=ALU.mult)))(),
                waits=list(act_waits), inc="dve")
            return aev

        def transpose_tile(src_fn, dst_fn, pe_waits, dve_waits=()):
            last = None
            for ch in range(c.NCF):
                w = list(pe_waits) + rwait(rd_psT) if ch == 0 else ()
                last = emit("pe", (lambda ch=ch: (lambda t: t.transpose(
                    psTb()[:, ch * 128:(ch + 1) * 128],
                    src_fn()[:, ch * 128:(ch + 1) * 128], ident_bf[:, :])))(),
                    waits=w, inc="pe")
            dv = emit("dve", (lambda: (lambda v: v.tensor_copy(
                dst_fn(), psTb()[:, 0:c.NCF * 128]
                .rearrange("p (ch t) -> p ch t", ch=c.NCF))))(),
                waits=list(dve_waits) + [("pe", last)], inc="dve")
            rset(rd_psT, "dve", dv)
            return dv, last

        p_iota = emit("pool", lambda g: g.iota(iota_sb[:, :], [[-1, 128]], base=0,
                                               channel_multiplier=1), inc="pool")
        emit("dve", lambda v: v.tensor_scalar(ident_bf[:, :], iota_sb[:, :], 0.0, None,
                                              op0=ALU.is_equal),
             waits=[("pool", p_iota)])
        for h in range(c.HP):
            emit("dve", (lambda h=h: (lambda v: v.tensor_scalar(
                mask_rep[:, h * 128:(h + 1) * 128], iota_sb[:, :], 0.0, None,
                op0=ALU.is_le)))())
        emit("dve", lambda v: v.memset(vprime[:, :], 1.0))
        emit("dve", lambda v: v.memset(kprime[64:65, :], -1.0))
        emit("dve", lambda v: v.memset(ones01[:, :], 1.0))
        emit("dve", lambda v: v.memset(ones1b[:, :], 1.0))
        emit("dve", lambda v: v.memset(onecol_b[:, :], 1.0))
        emit("dve", lambda v: v.memset(ones65r[:, :], 1.0))
        emit("dve", lambda v: v.memset(ones_r[:, :], 1.0))
        drain("dve")
        dv_init = emit("dve", lambda v: v.memset(onesb65[:, 0:1], 1.0), inc="dve")

        emit("pool", lambda g: g.dma_start(
            wk_sb[:, :].rearrange("p (m d) -> p m d", m=c.NCF),
            wk_d.rearrange("(m p) d -> p m d", p=128)), inc="swinit", amt=16)
        emit("pool", lambda g: g.dma_start(
            wv_sb[:, :].rearrange("p (m d) -> p m d", m=c.NCF),
            wv_d.rearrange("(m p) d -> p m d", p=128)), inc="swinit", amt=16)
        emit("pool", lambda g: g.dma_start(
            gate_sb[:, :].rearrange("p (m d) -> p m d", m=c.NCF),
            wgate_d.rearrange("(m p) d -> p m d", p=128)), inc="swinit", amt=16)
        emit("pool", lambda g: g.dma_start(bo_bf[0:1, :], bo_d), inc="swinit", amt=16)
        sw_small = cnt["swinit"]
        emit("sync", lambda sy: sy.dma_start(bg_row(), bgate_d), inc="srow", amt=16)
        sx_rows = cnt["srow"]

        pc = emit("pe", lambda t: t.matmul(psD[0:128, 0:c.E], ones01[0:1, 0:128],
                                           bg_row(), start=True, stop=True),
                  waits=[("srow", sx_rows), ("dve", dv_init)] + rwait(rd_psD), inc="pe")
        rset(rd_psD, "dve", emit("dve", lambda v: v.tensor_copy(bg_bc[:, :], psD[0:128, 0:c.E]),
                                 waits=[("pe", pc)], inc="dve"))

        dbg = getattr(c, "dbg", 0)
        skips = getattr(c, "skips", set())
        if dbg == 1:
            emit("sync", lambda sy: sy.nop(), waits=[("srow", cnt["srow"])])
            return finish()
        kv_ln = {}
        for t in range(c.NTILES):
            par = t % 2
            sxn = "sx" + str(par)
            dw = [("dve", kv_ln[t - 2])] if t >= 2 else []
            emit("sync", (lambda t=t, par=par: (lambda sy: sy.dma_start(
                out_sb[:, par * c.DM:(par + 1) * c.DM], x_d[t * 128:(t + 1) * 128, :])))(),
                waits=dw, inc=sxn, amt=16)
            sx_now = cnt[sxn]
            kv_ln[t] = ln_tile((lambda par=par: out_sb[:, par * c.DM:(par + 1) * c.DM]),
                               (lambda par=par: xn_tm[:, par * c.DM:(par + 1) * c.DM]),
                               dve_waits=[(sxn, sx_now)],
                               act_waits=rwait(rd_xn[par]))
            fmo = par * c.NCF * 128
            dvt, tr_pe = transpose_tile(
                (lambda par=par: xn_tm[:, par * c.DM:(par + 1) * c.DM]),
                (lambda fmo=fmo: xnq_fm[:, fmo:fmo + c.NCF * 128]
                 .rearrange("p (ch t) -> p ch t", ch=c.NCF)),
                pe_waits=[("dve", kv_ln[t])],
                dve_waits=rwait(rd_xnq))
            rset(rd_xn[par], "pe", tr_pe)
            lastk = None
            for ch in range(c.NCF):
                w = []
                if ch == 0:
                    w = [("dve", dvt), ("swinit", sw_small)] + rwait(rd_psD)
                lastk = emit("pe", (lambda ch=ch, fmo=fmo: (lambda t_: t_.matmul(
                    psD[0:64, 0:128], wk_sb[:, ch * c.HD:(ch + 1) * c.HD],
                    xnq_fm[:, fmo + ch * 128:fmo + (ch + 1) * 128],
                    start=(ch == 0), stop=(ch == c.NCF - 1))))(), waits=w, inc="pe")
            dvk = emit("dve", (lambda t=t: (lambda v: v.tensor_copy(
                kprime[0:64, t * 128:(t + 1) * 128], psD[0:64, 0:128])))(),
                waits=[("pe", lastk)], inc="dve")
            rset(rd_psD, "dve", dvk)
            if "k2" not in skips:
                a2 = emit("act", lambda s: s.activation(q2sb[0:64, 0:128], psD[0:64, 0:128],
                                                        AF.Square),
                          waits=[("pe", lastk), ("dve", dvk)] + rwait(rd_q2), inc="act")
                rset(rd_psD, "act", a2)
                if "bmm" in skips:
                    pk2 = None
                else:
                    pk2 = emit("pe", lambda t_: t_.matmul(psD[0:1, 256:384], onesb65[:, :],
                                                          q2sb[0:64, 0:128], start=True, stop=True),
                               waits=[("act", a2), ("dve", dvk)], inc="pe")
                    rset(rd_q2, "pe", pk2)
                if "red64" not in skips and pk2 is not None:
                    rset(rd_psD, "dve", emit("dve", (lambda t=t: (lambda v: v.reduce_max(
                        bb65[0:1, t:t + 1], psD[0:1, 256:384], axis=AX.X)))(),
                        waits=[("pe", pk2)], inc="dve"))
            lastv = None
            for ch in range(c.NCF):
                w = rwait(rd_psA) if ch == 0 else ()
                lastv = emit("pe", (lambda ch=ch, fmo=fmo: (lambda t_: t_.matmul(
                    psA[0:128, 0:64], xnq_fm[:, fmo + ch * 128:fmo + (ch + 1) * 128],
                    wv_sb[:, ch * c.HD:(ch + 1) * c.HD],
                    start=(ch == 0), stop=(ch == c.NCF - 1))))(), waits=w, inc="pe")
            rset(rd_xnq, "pe", lastv)
            rset(rd_psA, "dve", emit("dve", (lambda t=t: (lambda v: v.tensor_copy(
                vprime[:, t * 66:t * 66 + 64], psA[0:128, 0:64])))(),
                waits=[("pe", lastv)], inc="dve"))
        drain("dve")
        emit("sync", lambda sy: sy.dma_start(kprime[64:128, 0:c.SEQ], kprime[0:64, 0:c.SEQ]),
             waits=[("dve", cnt["dve"])], inc="sr", amt=16)
        st_kdup = cnt["sr"]
        KMAXC = max(c.NH, c.NTILES)
        if "k2" not in skips and "red64" not in skips:
            dv_kmax = emit("dve", lambda v: v.reduce_max(bb65[0:1, KMAXC:KMAXC + 1],
                                                         bb65[0:1, 0:c.NTILES], axis=AX.X),
                           inc="dve")
        else:
            dv_kmax = 0

        if dbg == 2:
            emit("sync", lambda sy: sy.nop(), waits=[("srow", cnt["srow"]), ("dve", cnt["dve"])])
            return finish()
        def qprime():
            return bigbf[0:65, 0:c.NH * c.HQ]

        def xn2_fm():
            return bigbf[0:128, 0:c.NCF * c.QUARTER]

        def wq_ap():
            return arena[:, 0:c.NCF * c.DM]

        def wo_ap():
            return arena[0:64, c.NCF * c.DM:c.NCF * c.DM + c.NH * c.DM]

        def eslot(s):
            return arena[:, s * c.SLOT:(s + 1) * c.SLOT]

        st = {
            "expert_pe": 0, "oproj_pe": 0, "qproj_pe": 0,
            "xres_free": 0, "eloads": [0, 0], "sr_n": 0, "r_pe": pc,
            "out_so": {0: 0, 1: 0}, "comb_done": 0, "qdup_rd": 0,
            "fin": [],
        }

        def flush_fin():
            fins = st["fin"]
            st["fin"] = []
            for f in fins:
                f()
        xres_add = {}
        xn2_dve = {}

        for q in range(c.NQUARTER):
            QT0 = q * c.NTQ

            ww = [("pe", st["expert_pe"])] if q > 0 else [("swinit", sw_small)]
            emit("pool", lambda g: g.dma_start(
                wq_ap().rearrange("p (m d) -> p m d", m=c.NCF),
                wq_d.rearrange("(m p) d -> p m d", p=128)), waits=ww, inc="sA", amt=16)
            emit("pool", lambda g: g.dma_start(
                wo_ap().rearrange("p (h d) -> p h d", h=c.NH),
                wo_d.rearrange("(h p) d -> p h d", p=64)), inc="sA", amt=16)
            sA_now = cnt["sA"]

            xw = [("dve", st["xres_free"])] if q > 0 else []
            for j in range(c.NTQ):
                gt = QT0 + j
                emit("sync", (lambda j=j, gt=gt: (lambda sy: sy.dma_start(
                    x_res[:, j * c.DM:(j + 1) * c.DM], x_d[gt * 128:(gt + 1) * 128, :])))(),
                    waits=(xw if j == 0 else ()), inc="sxq", amt=16)
            sx_xq = cnt["sxq"]

            for hb in range(c.NHQ):
                flush_fin()
                jt0 = hb * (c.HQ // 128)
                last_tr_dve = 0
                pend_t = None
                for i in range(c.HQ // 128 + 1):
                    if i < c.HQ // 128:
                        j = jt0 + i
                        par = i % 2
                        aev = ln_tile((lambda j=j: x_res[:, j * c.DM:(j + 1) * c.DM]),
                                      (lambda par=par: xn_tm[:, par * c.DM:(par + 1) * c.DM]),
                                      dve_waits=[("sxq", sx_xq)],
                                      act_waits=rwait(rd_xn[par]))
                        cur_t = (i, par, aev)
                    else:
                        cur_t = None
                    if pend_t is not None:
                        i2, par2, aev2 = pend_t
                        dst = (lambda i2=i2: xnq_fm[:, 0:c.NCF * c.HQ]
                               .rearrange("p (ch t) -> p ch t", ch=c.NCF)[:, :, i2 * 128:(i2 + 1) * 128])
                        dvt, lpe = transpose_tile(
                            (lambda par2=par2: xn_tm[:, par2 * c.DM:(par2 + 1) * c.DM]),
                            dst, pe_waits=[("dve", aev2)], dve_waits=rwait(rd_xnq))
                        rset(rd_xn[par2], "pe", lpe)
                        last_tr_dve = dvt
                    pend_t = cur_t
                for h in range(c.NH):
                    buf = h % 2
                    lastq = None
                    for ch in range(c.NCF):
                        w = []
                        if ch == 0:
                            w = [("dve", last_tr_dve), ("sA", sA_now)] + rwait(rd_psC[buf])
                        lastq = emit("pe", (lambda ch=ch, h=h, buf=buf: (lambda t: t.matmul(
                            psC[0:64, buf * 512:buf * 512 + c.HQ],
                            arena[:, ch * c.DM + h * c.HD:ch * c.DM + (h + 1) * c.HD],
                            xnq_fm[:, ch * c.HQ:(ch + 1) * c.HQ],
                            start=(ch == 0), stop=(ch == c.NCF - 1))))(), waits=w, inc="pe")
                    st["qproj_pe"] = lastq
                    rset(rd_xnq, "pe", lastq)
                    aq = emit("act", (lambda h=h, buf=buf: (lambda s: s.activation(
                        qprime()[0:64, h * c.HQ:(h + 1) * c.HQ],
                        psC[0:64, buf * 512:buf * 512 + c.HQ], AF.Identity,
                        scale=f32(c.HD ** -0.5))))(),
                        waits=[("pe", lastq)] + rwait(rd_big), inc="act")
                    a2 = emit("act", (lambda buf=buf: (lambda s: s.activation(
                        q2sb[0:64, 0:c.HQ], psC[0:64, buf * 512:buf * 512 + c.HQ],
                        AF.Square)))(), waits=rwait(rd_q2), inc="act")
                    rset(rd_psC[buf], "act", a2)
                    dvqm = emit("dve", (lambda h=h: (lambda v: v.reduce_max(
                        q2m[0:64, h:h + 1], q2sb[0:64, 0:c.HQ], axis=AX.X)))(),
                        waits=[("act", a2)], inc="dve")
                    rset(rd_q2, "dve", dvqm)
                pb = emit("pe", (lambda: (lambda t: t.matmul(
                    psD[0:1, 0:c.NH], onesb65[:, :], q2m[0:64, 0:c.NH],
                    start=True, stop=True)))(),
                    waits=[("dve", dvqm)] + rwait(rd_psD), inc="pe")
                rset(rd_psD, "dve", emit("dve", (lambda pb=pb: (lambda v: v.reduce_max(
                    bb65[0:1, KMAXC + 1:KMAXC + 2], psD[0:1, 0:c.NH], axis=AX.X)))(),
                    waits=[("pe", pb)], inc="dve"))
                drain("dve")
                dvb = emit("dve", lambda v: v.tensor_tensor(
                    bb65[0:1, KMAXC + 2:KMAXC + 3], bb65[0:1, KMAXC + 1:KMAXC + 2],
                    bb65[0:1, KMAXC:KMAXC + 1], op=ALU.mult),
                    waits=[("dve", dv_kmax)], inc="dve")
                ab = emit("act", lambda s: s.activation(
                    bb65[0:1, KMAXC + 3:KMAXC + 4], bb65[0:1, KMAXC + 2:KMAXC + 3],
                    AF.Sqrt, scale=f32((c.HD ** -1.0) * 1.0609)), waits=[("dve", dvb)],
                    inc="act")
                pb0 = emit("pe", lambda t: t.matmul(
                    psA[0:128, 0:1], ones01[0:1, 0:128],
                    bb65[0:1, KMAXC + 3:KMAXC + 4],
                    start=True, stop=True),
                    waits=[("act", ab)] + rwait(rd_psA), inc="pe")
                dvq = emit("dve", lambda v: v.tensor_scalar(
                    nb_bc[:, 0:1], psA[0:128, 0:1], -1.0, None, op0=ALU.mult),
                    waits=[("pe", pb0)], inc="dve")
                rset(rd_psA, "dve", dvq)
                emit("sync", (lambda: (lambda sy: sy.dma_start(
                    bigbf[64:128, 0:c.NH * c.HQ], bigbf[0:64, 0:c.NH * c.HQ])))(),
                    waits=[("act", cnt["act"]), ("pe", st["qdup_rd"]), ("sr", st_kdup)],
                    inc="sr", amt=16)
                st_qdup = cnt["sr"]

                def oproj_body(j):
                    for hh in range(c.NDH):
                        lasto = None
                        for h in range(c.NH):
                            w = []
                            if h == 0:
                                w = [("dve", st["attnevac"]), ("sA", sA_now)] + rwait(rd_psD)
                            wob = c.NCF * c.DM
                            lasto = emit("pe", (lambda h=h, hh=hh, wob=wob: (lambda t: t.matmul(
                                psD[0:128, 0:c.DH],
                                attn_dm()[:, h * 128:(h + 1) * 128],
                                arena[0:64, wob + h * c.DM + hh * c.DH:
                                      wob + h * c.DM + (hh + 1) * c.DH],
                                start=(h == 0), stop=False)))(), waits=w, inc="pe")
                        lasto = emit("pe", (lambda hh=hh: (lambda t: t.matmul(
                            psD[0:128, 0:c.DH], ones1b[0:1, 0:128],
                            bo_bf[0:1, hh * c.DH:(hh + 1) * c.DH],
                            start=False, stop=True)))(), inc="pe")
                        st["oproj_pe"] = lasto
                        rset(rd_attn, "pe", lasto)
                        for m_ in range(c.NCI):
                            if m_ * c.TB * 4 < c.NH * 128 * 2:
                                rset(rd_hg[m_], "pe", lasto)
                        rset(rd_psD, "act", emit("act", (lambda hh=hh: (lambda s: s.activation(
                            oz()[:, hh * c.DH:(hh + 1) * c.DH], psD[0:128, 0:c.DH],
                            AF.Identity)))(),
                            waits=[("pe", lasto)] + rwait(rd_racc), inc="act"))
                    drain("dve")
                    aeo = ln_tile(lambda: oz()[:, :], lambda: nrm[:, :],
                                  dve_waits=[("act", cnt["act"])],
                                  act_waits=rwait(rd_nrm))
                    rset(rd_racc, "dve", aeo)
                    rset(rd_racc, "act", 0)
                    drain("dve")
                    xres_add[j] = emit("dve", (lambda j=j: (lambda v: v.tensor_tensor(
                        x_res[:, j * c.DM:(j + 1) * c.DM], x_res[:, j * c.DM:(j + 1) * c.DM],
                        nrm[:, :], op=ALU.add)))(), inc="dve")
                    rset(rd_nrm, "dve", xres_add[j])

                for i in range(c.HQ // 128):
                    j = jt0 + i
                    gt = QT0 + j
                    for pp in range(c.NPASS):
                        avp = st["avp"] = st.get("avp", 0) ^ 1
                        pend = []
                        avlast = None

                        def emit_scores(kt, pp=pp, i=i):
                            buf = kt % 4
                            ps_sc = psA if buf % 2 == 0 else psB
                            rd_sc = rd_psA if buf % 2 == 0 else rd_psB
                            rp = 64 * (kt % 2)
                            lasts = None
                            for mmi in range(c.NMM):
                                h0 = pp * c.HP + mmi * c.HM
                                w = []
                                if mmi == 0:
                                    w = [("dve", dvq), ("sr", st_qdup)] + rwait(rd_sc)
                                lasts = emit("pe", (lambda kt=kt, h0=h0, mmi=mmi, rp=rp,
                                                    ps_sc=ps_sc, i=i: (lambda t: t.matmul(
                                    ps_sc[0:128, mmi * c.HM * 128:(mmi + 1) * c.HM * 128],
                                    kprime[rp:rp + 64, kt * 128:(kt + 1) * 128],
                                    bigbf[0:128, 0:c.NH * c.HQ]
                                    .rearrange("p (h t) -> p h t", h=c.NH)
                                    [rp:rp + 64, h0:h0 + c.HM, i * 128:(i + 1) * 128],
                                    start=True, stop=True,
                                    tile_position=(rp, 0) if rp else None)))(),
                                    waits=w, inc="pe")
                            if rp:
                                st["qdup_rd"] = lasts
                            rset(rd_big, "pe", lasts)
                            ax = emit("act", (lambda buf=buf, ps_sc=ps_sc: (lambda s: s.activation(
                                hp_ar[:, buf * c.HP * 128:(buf + 1) * c.HP * 128],
                                ps_sc[0:128, 0:c.HP * 128], AF.Exp,
                                bias=nb_bc[:, 0:1])))(),
                                waits=[("pe", lasts)] + rwait(rd_P[buf]), inc="act")
                            rset(rd_sc, "act", ax)
                            pwait = ("act", ax)
                            if kt == gt:
                                dvm = emit("dve", (lambda buf=buf: (lambda v: v.tensor_tensor(
                                    hp_ar[:, buf * c.HP * 128:(buf + 1) * c.HP * 128],
                                    hp_ar[:, buf * c.HP * 128:(buf + 1) * c.HP * 128],
                                    mask_rep[:, 0:c.HP * 128], op=ALU.mult)))(),
                                    waits=[("act", ax)], inc="dve")
                                pwait = ("dve", dvm)
                            return (buf, pwait, kt)

                        def emit_av(ent, gt=gt):
                            nonlocal avlast
                            pbuf, pwait, pkt = ent
                            avl = None
                            for g2 in range(c.NMM):
                                w = [pwait]
                                if pkt == 0 and g2 == 0:
                                    w += rwait(rd_psC[0], rd_psC[1])
                                avl = emit("pe", (lambda pkt=pkt, g2=g2, pbuf=pbuf, gt=gt:
                                                  (lambda t: t.matmul(
                                    psC[0:65, g2 * c.HM * 128:(g2 + 1) * c.HM * 128],
                                    vprime[:, pkt * 66:pkt * 66 + 65],
                                    hp_ar[:, pbuf * c.HP * 128 + g2 * c.HM * 128:
                                           pbuf * c.HP * 128 + (g2 + 1) * c.HM * 128],
                                    start=(pkt == 0), stop=(pkt == gt),
                                    skip_group_check=True)))(), waits=w, inc="pe")
                            rset(rd_P[pbuf], "pe", avl)
                            avlast = avl

                        for k0 in range(0, gt + 1, 2):
                            while len(pend) >= 3:
                                emit_av(pend.pop(0))
                            pend.append(emit_scores(k0))
                            if k0 + 1 <= gt:
                                pend.append(emit_scores(k0 + 1))
                            if k0 == min(4, 2 * (gt // 2)):
                                flush_fin()
                        for ent in pend:
                            emit_av(ent)
                        dvav = emit("dve", (lambda avp=avp: (lambda v: v.tensor_copy(
                            av_scr[0:65, avp * c.HP * 128:(avp + 1) * c.HP * 128],
                            psC[0:65, 0:c.HP * 128])))(),
                            waits=[("pe", avlast)] + rwait(rd_avs[avp]), inc="dve")
                        rset(rd_psC[0], "dve", dvav)
                        rset(rd_psC[1], "dve", dvav)

                        def fin_pass(pp=pp, avp=avp):
                            drain("dve")
                            dvr = emit("dve", (lambda avp=avp: (lambda v: v.reciprocal(
                                r65[64:65, 0:c.HP * 128],
                                av_scr[64:65, avp * c.HP * 128:(avp + 1) * c.HP * 128])))(),
                                waits=[("pe", st["r_pe"])], inc="dve", lp=True)
                            for rj in range(c.NMM):
                                lo = rj * c.HM * 128
                                prb = emit("pe", (lambda rj=rj, lo=lo: (lambda t: t.matmul(
                                    psD[0:64, 0:c.HM * 128], ones65r[64:65, 0:64],
                                    r65[64:65, lo:lo + c.HM * 128],
                                    start=True, stop=True, tile_position=(64, 0))))(),
                                    waits=[("dve", dvr)] + rwait(rd_psD), inc="pe")
                                st["r_pe"] = prb
                                rset(rd_psD, "dve", emit("dve", (lambda lo=lo, prb=prb:
                                                                 (lambda v: v.tensor_copy(
                                    rb_sb[0:64, lo:lo + c.HM * 128],
                                    psD[0:64, 0:c.HM * 128])))(),
                                    waits=[("pe", prb)], inc="dve"))
                            drain("dve")
                            dve_ae = emit("dve", (lambda pp=pp, avp=avp: (lambda v: v.tensor_tensor(
                                attn_dm()[:, pp * c.HP * 128:(pp + 1) * c.HP * 128],
                                av_scr[0:64, avp * c.HP * 128:(avp + 1) * c.HP * 128],
                                rb_sb[:, :], op=ALU.mult)))(),
                                waits=[("pe", st["r_pe"])] + rwait(rd_attn), inc="dve")
                            rset(rd_avs[avp], "dve", dve_ae)
                            st["attnevac"] = dve_ae
                        st["fin"].append(fin_pass)
                    def fin_qtile(j=j):
                        oproj_body(j)
                    st["fin"].append(fin_qtile)

            flush_fin()
            lastflash_pe = cnt["pe"]

            pend_t = None
            for j in range(c.NTQ + 1):
                if j < c.NTQ:
                    par = j % 2
                    drain("dve")
                    aev = ln_tile((lambda j=j: x_res[:, j * c.DM:(j + 1) * c.DM]),
                                  (lambda par=par: xn_tm[:, par * c.DM:(par + 1) * c.DM]),
                                  dve_waits=[("dve", xres_add[j])],
                                  act_waits=rwait(rd_xn[par]))
                    cur_t = (j, par, aev)
                else:
                    cur_t = None
                if pend_t is None:
                    pend_t = cur_t
                    continue
                j, par, aev = pend_t
                pend_t = cur_t
                dst = (lambda j=j: bigbf[0:128, 0:c.NCF * c.QUARTER]
                       .rearrange("p (ch t) -> p ch t", ch=c.NCF)[:, :, j * 128:(j + 1) * 128])
                dvt, lpe = transpose_tile(
                    (lambda par=par: xn_tm[:, par * c.DM:(par + 1) * c.DM]),
                    dst, pe_waits=[("dve", aev), ("pe", lastflash_pe)],
                    dve_waits=rwait(rd_big))
                rset(rd_xn[par], "pe", lpe)
                xn2_dve[j] = dvt
                lastg = None
                for ch in range(c.NCF):
                    w = []
                    if ch == 0:
                        w = [("dve", dvt)] + rwait(rd_psD)
                    lastg = emit("pe", (lambda ch=ch, j=j: (lambda t: t.matmul(
                        psD[0:128, 0:c.E],
                        bigbf[0:128, ch * c.QUARTER + j * 128:ch * c.QUARTER + (j + 1) * 128],
                        gate_sb[:, ch * c.E:(ch + 1) * c.E],
                        start=(ch == 0), stop=(ch == c.NCF - 1))))(), waits=w, inc="pe")
                rset(rd_big, "pe", lastg)
                sig = emit("act", (lambda j=j: (lambda s: s.activation(
                    sc_r[:, j * c.E:(j + 1) * c.E], psD[0:128, 0:c.E], AF.Sigmoid)))(),
                    waits=[("pe", lastg), ("dve", st["comb_done"])], inc="act")
                rset(rd_psD, "act", sig)
                st["sig_%d" % j] = sig
            T_ = c.NTQ

            def rstep(fn, waits=()):
                for j in range(T_):
                    emit("dve", (lambda j=j: (lambda v: fn(v, j)))(),
                         waits=(waits if j == 0 else ()))
                drain("dve")

            for j in range(T_):
                emit("dve", (lambda j=j: (lambda v: v.tensor_tensor(
                    sc_r[:, j * c.E:(j + 1) * c.E], sc_r[:, j * c.E:(j + 1) * c.E],
                    bg_bc[:, :], op=ALU.add)))(),
                    waits=[("act", st["sig_%d" % j])])
            drain("dve")
            for g2 in range(c.G):
                rstep(lambda v, j, g2=g2: v.reduce_max(
                    gs_r[:, j * c.G + g2:j * c.G + g2 + 1],
                    sc_r[:, j * c.E + g2 * c.EPG:j * c.E + (g2 + 1) * c.EPG], axis=AX.X))
            rstep(lambda v, j: v.tensor_copy(gg_r[:, j * c.G:(j + 1) * c.G],
                                             gs_r[:, j * c.G:(j + 1) * c.G]))
            for it in range(c.TG):
                rstep(lambda v, j: v.reduce_max(mcol[:, j:j + 1],
                                                gg_r[:, j * c.G:(j + 1) * c.G], axis=AX.X))
                if it < c.TG - 1:
                    rstep(lambda v, j: v.tensor_scalar(
                        gw_r[:, j * c.G:(j + 1) * c.G], gg_r[:, j * c.G:(j + 1) * c.G],
                        mcol[:, j:j + 1], None, op0=ALU.is_equal))
                    rstep(lambda v, j: v.scalar_tensor_tensor(
                        gg_r[:, j * c.G:(j + 1) * c.G], gw_r[:, j * c.G:(j + 1) * c.G],
                        -1e30, gg_r[:, j * c.G:(j + 1) * c.G], op0=ALU.mult, op1=ALU.add))
            rstep(lambda v, j: v.tensor_scalar(
                gs_r[:, j * c.G:(j + 1) * c.G], gs_r[:, j * c.G:(j + 1) * c.G],
                mcol[:, j:j + 1], None, op0=ALU.is_ge))
            for g2 in range(c.G):
                rstep(lambda v, j, g2=g2: v.tensor_scalar(
                    gw_r[:, j * c.E + g2 * c.EPG:j * c.E + (g2 + 1) * c.EPG],
                    ones_r[:, 0:c.EPG], gs_r[:, j * c.G + g2:j * c.G + g2 + 1], None,
                    op0=ALU.mult))
            rstep(lambda v, j: v.tensor_tensor(
                gw_r[:, j * c.E:(j + 1) * c.E], gw_r[:, j * c.E:(j + 1) * c.E],
                sc_r[:, j * c.E:(j + 1) * c.E], op=ALU.mult))
            rstep(lambda v, j: v.tensor_copy(comb[:, j * c.E:(j + 1) * c.E],
                                             gw_r[:, j * c.E:(j + 1) * c.E]),
                  waits=rwait(rd_racc))
            for it in range(c.TK):
                rstep(lambda v, j: v.reduce_max(mcol[:, j:j + 1],
                                                comb[:, j * c.E:(j + 1) * c.E], axis=AX.X))
                if it < c.TK - 1:
                    rstep(lambda v, j: v.tensor_scalar(
                        sc_r[:, j * c.E:(j + 1) * c.E], comb[:, j * c.E:(j + 1) * c.E],
                        mcol[:, j:j + 1], None, op0=ALU.is_equal))
                    rstep(lambda v, j: v.scalar_tensor_tensor(
                        comb[:, j * c.E:(j + 1) * c.E], sc_r[:, j * c.E:(j + 1) * c.E],
                        -1e30, comb[:, j * c.E:(j + 1) * c.E], op0=ALU.mult, op1=ALU.add))
            rstep(lambda v, j: v.tensor_scalar(
                sc_r[:, j * c.E:(j + 1) * c.E], gw_r[:, j * c.E:(j + 1) * c.E],
                mcol[:, j:j + 1], None, op0=ALU.is_ge))
            rstep(lambda v, j: v.tensor_tensor(
                comb[:, j * c.E:(j + 1) * c.E], gw_r[:, j * c.E:(j + 1) * c.E],
                sc_r[:, j * c.E:(j + 1) * c.E], op=ALU.mult))
            rstep(lambda v, j: v.reduce_sum(mcol[:, j:j + 1],
                                            comb[:, j * c.E:(j + 1) * c.E], axis=AX.X))
            rstep(lambda v, j: v.tensor_scalar(mcol[:, j:j + 1], mcol[:, j:j + 1],
                                               1e-8, None, op0=ALU.add))
            rstep(lambda v, j: v.reciprocal(mcol[:, j:j + 1], mcol[:, j:j + 1]))
            rstep(lambda v, j: v.tensor_scalar(
                comb[:, j * c.E:(j + 1) * c.E], comb[:, j * c.E:(j + 1) * c.E],
                mcol[:, j:j + 1], None, op0=ALU.mult))
            st["comb_done"] = cnt["dve"]

            dn_pe = {}
            evac_dve = {}
            silu_a = {}
            hp_dve = {}
            gch, uch, dnch = [0], [0], [0]
            hp_rd_pe = {(tp_, m): lastflash_pe for tp_ in range(2)
                        for m in range(c.NCI)}
            last_hp = {}

            for se in range(c.NSLOT):
                p = se % 2
                sE = "sE" + str(p)
                if se >= 2:
                    ww = [("pe", dn_pe[se - 2])]
                else:
                    ww = [("pe", st["oproj_pe"]), ("pe", st["qproj_pe"])]
                if se < 2:
                    srcs = [wsg_d[:, se * c.INTER:(se + 1) * c.INTER],
                            wsu_d[:, se * c.INTER:(se + 1) * c.INTER],
                            wsd_d[se * c.INTER:(se + 1) * c.INTER, :]]
                else:
                    srcs = [weg_d[se - 2], weu_d[se - 2], wed_d[se - 2]]
                for mat in range(3):
                    src = srcs[mat]
                    nch = c.NCF if mat < 2 else c.NCI
                    emit("pool", (lambda p=p, mat=mat, src=src, nch=nch: (lambda g: g.dma_start(
                        eslot(p * 3 + mat).rearrange("p (ch i) -> p ch i", ch=nch),
                        src.rearrange("(ch p) i -> p ch i", p=128))))(),
                        waits=(ww if mat == 0 else ()), inc=sE, amt=16)
                st["eloads"][p] += 1
                sE_now = cnt[sE]

                for tb in range(c.NTB):
                    for mat in range(2):
                        pgu = psA if mat == 0 else psB
                        rd_gu = rd_psA if mat == 0 else rd_psB
                        cidx = gch if mat == 0 else uch
                        for m in range(c.NCI):
                            half = m % 2
                            w = []
                            if m == 0:
                                w = [(sE, sE_now)]
                                if se < 2 and tb == 0:
                                    w.append(("dve", xn2_dve[c.NTQ - 1]))
                            w = w + (rwait(rd_gu) if (cidx[0] % 2 == 0 or cidx[0] < 2) else [])
                            prev = cidx[0] - 2
                            if prev >= 0:
                                if mat == 0:
                                    w.append(("act", silu_a[prev]))
                                else:
                                    w.append(("dve", hp_dve[prev]))
                            else:
                                w += rwait(rd_gu)
                            lastm = None
                            for ch in range(c.NCF):
                                lastm = emit("pe", (lambda ch=ch, m=m, half=half, tb=tb,
                                                    p=p, mat=mat, pgu=pgu: (lambda t: t.matmul(
                                    pgu[0:128, half * 512:half * 512 + c.TB],
                                    eslot(p * 3 + mat)[:, ch * c.INTER + m * 128:
                                                       ch * c.INTER + (m + 1) * 128],
                                    bigbf[0:128, ch * c.QUARTER + tb * c.TB:
                                          ch * c.QUARTER + (tb + 1) * c.TB],
                                    start=(ch == 0), stop=(ch == c.NCF - 1))))(),
                                    waits=(w if ch == 0 else ()), inc="pe")
                            rset(rd_big, "pe", lastm)
                            if mat == 0:
                                silu_a[cidx[0]] = emit("act", (lambda m=m, half=half:
                                                               (lambda s: s.activation(
                                    hg[:, m * c.TB:(m + 1) * c.TB],
                                    psA[0:128, half * 512:half * 512 + c.TB], AF.Silu)))(),
                                    waits=[("pe", lastm)] + rwait(rd_hg[m]) +
                                          (rwait(rd_attn) if m * c.TB * 4 < c.NH * 128 * 2 else []),
                                    inc="act")
                                rset(rd_psA, "act", silu_a[cidx[0]])
                            else:
                                tpar = tb % 2
                                hp_dve[cidx[0]] = emit("dve", (lambda m=m, half=half, tpar=tpar:
                                                               (lambda v: v.tensor_tensor(
                                    hp_ar[:, tpar * c.NCI * c.TB + m * c.TB:
                                           tpar * c.NCI * c.TB + (m + 1) * c.TB],
                                    hg[:, m * c.TB:(m + 1) * c.TB],
                                    psB[0:128, half * 512:half * 512 + c.TB],
                                    op=ALU.mult)))(),
                                    waits=[("pe", lastm), ("act", silu_a[cidx[0]]),
                                           ("pe", hp_rd_pe[(tb % 2, m)])],
                                    inc="dve")
                                last_hp[(tb % 2, m)] = hp_dve[cidx[0]]
                                rset(rd_psB, "dve", hp_dve[cidx[0]])
                                rset(rd_hg[m], "dve", hp_dve[cidx[0]])
                            cidx[0] += 1
                    hp_base = uch[0] - c.NCI
                    for tt in range(c.NTT):
                        for hh in range(c.NDH):
                            slot4 = dnch[0] % 4
                            if slot4 == 0:
                                dn_ps, dn_rd = psC, rd_psC[0]
                                dn_lo = 0
                            elif slot4 == 1:
                                dn_ps, dn_rd = psC, rd_psC[1]
                                dn_lo = 512
                            elif slot4 == 2:
                                dn_ps, dn_rd = psD, rd_psD
                                dn_lo = 0
                            else:
                                dn_ps, dn_rd = psT, rd_psT
                                dn_lo = 0
                            w = []
                            prev = dnch[0] - 4
                            if prev >= 0:
                                w.append(("dve", evac_dve[prev]))
                            else:
                                w += rwait(dn_rd)
                            lastd = None
                            tpar = tb % 2
                            for ic in range(c.NCI):
                                wic = list(w) if ic == 0 else []
                                wic.append(("dve", hp_dve[hp_base + ic]))
                                lastd = emit("pe", (lambda ic=ic, tt=tt, hh=hh, dn_ps=dn_ps,
                                                    dn_lo=dn_lo, tpar=tpar, p=p: (lambda t: t.matmul(
                                    dn_ps[0:128, dn_lo:dn_lo + c.DH],
                                    hp_ar[:, tpar * c.NCI * c.TB + ic * c.TB + tt * 128:
                                           tpar * c.NCI * c.TB + ic * c.TB + (tt + 1) * 128],
                                    eslot(p * 3 + 2)[:, ic * c.DM + hh * c.DH:
                                                     ic * c.DM + (hh + 1) * c.DH],
                                    start=(ic == 0), stop=(ic == c.NCI - 1))))(),
                                    waits=wic, inc="pe")
                            dn_pe[se] = lastd
                            tile = tb * c.NTT + tt
                            roff = tile * c.DM + hh * c.DH
                            ew = [("pe", lastd)]
                            if se == 0:
                                ew += rwait(rd_racc)
                            if tt == 0 and hh == 0:
                                drain("dve")
                            if se == 0:
                                evac_dve[dnch[0]] = emit("dve", (lambda roff=roff, dn_ps=dn_ps,
                                                                 dn_lo=dn_lo:
                                                                 (lambda v: v.tensor_copy(
                                    racc[:, roff:roff + c.DH],
                                    dn_ps[0:128, dn_lo:dn_lo + c.DH])))(),
                                    waits=ew, inc="dve")
                            elif se == 1:
                                evac_dve[dnch[0]] = emit("dve", (lambda roff=roff, dn_ps=dn_ps,
                                                                 dn_lo=dn_lo:
                                                                 (lambda v: v.tensor_tensor(
                                    racc[:, roff:roff + c.DH],
                                    dn_ps[0:128, dn_lo:dn_lo + c.DH],
                                    racc[:, roff:roff + c.DH], op=ALU.add)))(),
                                    waits=ew, inc="dve")
                            else:
                                ccol = tile * c.E + (se - 2)
                                evac_dve[dnch[0]] = emit("dve", (lambda roff=roff, dn_ps=dn_ps,
                                                                 dn_lo=dn_lo, ccol=ccol: (lambda v:
                                                                 v.scalar_tensor_tensor(
                                    racc[:, roff:roff + c.DH],
                                    dn_ps[0:128, dn_lo:dn_lo + c.DH],
                                    comb[:, ccol:ccol + 1],
                                    racc[:, roff:roff + c.DH],
                                    op0=ALU.mult, op1=ALU.add)))(),
                                    waits=ew, inc="dve")
                            rset(dn_rd, "dve", evac_dve[dnch[0]])
                            dnch[0] += 1
                        for m_ in range(c.NCI):
                            hp_rd_pe[(tpar, m_)] = dn_pe[se]


            st["expert_pe"] = dn_pe[c.NSLOT - 1]
            for b_ in range(4):
                rset(rd_P[b_], "pe", st["expert_pe"])
            last_evac = evac_dve[dnch[0] - 1]

            for j in range(c.NTQ):
                par = j % 2
                drain("dve")
                aev = ln_tile((lambda j=j: racc[:, j * c.DM:(j + 1) * c.DM]),
                              (lambda: nrm[:, :]),
                              dve_waits=[("dve", last_evac)],
                              act_waits=rwait(rd_nrm))
                rset(rd_racc, "dve", aev)
                drain("dve")
                dvo = emit("dve", (lambda j=j, par=par: (lambda v: v.tensor_tensor(
                    out_sb[:, par * c.DM:(par + 1) * c.DM],
                    x_res[:, j * c.DM:(j + 1) * c.DM], nrm[:, :], op=ALU.add)))(),
                    waits=[("so" + str(par), st["out_so"][par])], inc="dve")
                rset(rd_nrm, "dve", dvo)
                rset(rd_racc, "dve", dvo)
                gt = QT0 + j
                emit("sync", (lambda gt=gt, par=par: (lambda sy: sy.dma_start(
                    out_d[gt * 128:(gt + 1) * 128, :],
                    out_sb[:, par * c.DM:(par + 1) * c.DM])))(),
                    waits=[("dve", dvo)], inc="so" + str(par), amt=16)
                st["out_so"][par] = cnt["so" + str(par)]
            st["xres_free"] = cnt["dve"]

        emit("sync", lambda sy: sy.nop(),
             waits=[("so0", cnt["so0"]), ("so1", cnt["so1"])])
        return finish()

    return nc


def _layout(inputs, c: Cfg):
    ac = np.ascontiguousarray
    return {
        "wq": ac(inputs["L0_attn_Wq"]),
        "wk": ac(inputs["L0_attn_Wk"]),
        "wv": ac(inputs["L0_attn_Wv"]),
        "wo": ac(inputs["L0_attn_Wo"]),
        "bo": ac(inputs["L0_attn_bo"]).reshape(1, c.DM),
        "wgate": ac(inputs["L0_ffn_gate_w"].T),
        "bgate": ac(inputs["L0_ffn_gate_bias"]).reshape(1, c.E),
        "weg": ac(inputs["L0_ffn_eg_w"].transpose(0, 2, 1)),
        "weu": ac(inputs["L0_ffn_eu_w"].transpose(0, 2, 1)),
        "wed": ac(inputs["L0_ffn_ed_w"].transpose(0, 2, 1)),
        "wsg": ac(inputs["L0_ffn_sg_w"].T),
        "wsu": ac(inputs["L0_ffn_su_w"].T),
        "wsd": ac(inputs["L0_ffn_sd_w"].T),
    }


def kernel(**inputs):
    c = FULL
    x = np.asarray(inputs["L0_attn_x"], dtype=np.float32)
    B = x.shape[0]
    shared = _layout({k: np.asarray(v, dtype=np.float32) for k, v in inputs.items()}, c)
    nc = build(c)
    in_maps = []
    for b in range(B):
        m = dict(shared)
        m["x"] = np.ascontiguousarray(x[b])
        in_maps.append(m)
    res = run_bass_kernel_spmd(nc, in_maps, core_ids=list(range(B)))
    out = np.stack([np.asarray(r["out"]) for r in res.results], axis=0)
    return out.astype(np.float32)
